# version 02
# baseline (speedup 1.0000x reference)
import numpy as np
import ml_dtypes
from contextlib import ExitStack

import concourse.bass as bass
import concourse.mybir as mybir
from concourse.bass_utils import run_bass_kernel_spmd

F32 = mybir.dt.float32
BF16 = mybir.dt.bfloat16
AF = mybir.ActivationFunctionType
ALU = mybir.AluOpType
AX = mybir.AxisListType
BF = ml_dtypes.bfloat16

NCORE = 8
E = 2048
TOK = 1024
H = 16
NEC = 16
NE = 8
I1 = 1408
NOC = 22
NIC = 11
SCALE = float(1.0 / np.sqrt(128.0))
SHIFT = -30.0
EPS = 1e-5
NSLOT = [16, 32, 48, 64]
SLOT0 = [0, 16, 48, 96]
NSLOT_TOT = 160

ENGS = ("sync", "gpsimd", "scalar", "vector", "tensor")
import os as _os
QK_HEADS = int(_os.environ.get("QK_HEADS", "16"))
P3_LN = _os.environ.get("P3_LN", "full")
DEBUG_TAPS = tuple(x for x in _os.environ.get("DEBUG_TAPS", "").split(",") if x)
QK_STAGE = _os.environ.get("QK_STAGE", "full")
P0_MODE = _os.environ.get("P0_MODE", "full")


class Sem:
    __slots__ = ("h", "n", "name")

    def __init__(self, h, name):
        self.h = h
        self.n = 0
        self.name = name


class Build:
    def __init__(self, nc, stack):
        self.nc = nc
        self.stack = stack
        self.q = {e: [] for e in ENGS}
        self.sems = []
        self.waits = []
        self.marks = []

    def mark(self):
        self.marks.append({e: len(self.q[e]) for e in ENGS})

    def sem(self, name):
        s = Sem(self.stack.enter_context(self.nc.semaphore(name)), name)
        self.sems.append(s)
        return s

    def wait(self, eng, sem, tgt=None):
        v = sem.n if tgt is None else tgt
        if v <= 0:
            return
        self.waits.append((sem, v))
        self.q[eng].append(lambda e, h=sem.h, v=v: e.wait_ge(h, v))

    def dma(self, eng, sem, out_ap, in_ap, **kw):
        sem.n += 16
        self.q[eng].append(
            lambda e, o=out_ap, i=in_ap, h=sem.h, kw=kw:
                e.dma_start(out=o, in_=i, **kw).then_inc(h, 16))
        return sem.n

    def op(self, eng, method, *args, sem=None, sem_bare=False, **kw):
        tgt = None
        if sem is not None:
            sem.n += 1
            tgt = sem.n

        def f(e, m=method, a=args, kw=kw, s=sem, bare=sem_bare):
            ins = getattr(e, m)(*a, **kw)
            if s is not None:
                if bare:
                    ins.then_inc(s.h)
                else:
                    ins.then_inc(s.h, 1)
        self.q[eng].append(f)
        return tgt

    def gate(self, engs=ENGS, exclude=()):
        snap = [(s, s.n) for s in self.sems if s.n > 0 and s not in exclude]
        for eng in engs:
            for s, v in snap:
                self.wait(eng, s, v)

    def audit(self):
        for s, v in self.waits:
            assert v <= s.n, f"wait target {v} > final {s.n} on sem {s.name}"

    def emit(self, cut=None):
        blk = self.stack.enter_context(self.nc.Block())
        for eng in ENGS:
            fns = self.q[eng]
            if cut is not None and cut < len(self.marks):
                fns = fns[:self.marks[cut][eng]]
            getattr(blk, eng)(lambda e, fns=fns: [f(e) for f in fns])


def build_nc(cut=None):
    nc = bass.Bass("TRN2", target_bir_lowering=False, debug=False)

    dx = nc.dram_tensor("x", [TOK, E], F32, kind="ExternalInput").ap()
    dwqt = nc.dram_tensor("wqt", [E, E], F32, kind="ExternalInput").ap()
    dwkt = nc.dram_tensor("wkt", [E, E], F32, kind="ExternalInput").ap()
    dwvt = nc.dram_tensor("wvt", [E, E], F32, kind="ExternalInput").ap()
    dwot = nc.dram_tensor("wot", [E, E], F32, kind="ExternalInput").ap()
    dbq = nc.dram_tensor("bq", [1, E], F32, kind="ExternalInput").ap()
    dbk = nc.dram_tensor("bk", [1, E], F32, kind="ExternalInput").ap()
    dbv = nc.dram_tensor("bv", [1, E], F32, kind="ExternalInput").ap()
    dbot = nc.dram_tensor("bot", [128, 16], F32, kind="ExternalInput").ap()
    dgwt = nc.dram_tensor("gwt", [128, 128], F32, kind="ExternalInput").ap()
    dw1t = nc.dram_tensor("w1t", [NE, E, 2 * I1], F32, kind="ExternalInput").ap()
    dw2t = nc.dram_tensor("w2t", [NE, I1, E], F32, kind="ExternalInput").ap()
    dcos = nc.dram_tensor("cosx", [128, TOK], F32, kind="ExternalInput").ap()
    dsin = nc.dram_tensor("sinx", [128, TOK], F32, kind="ExternalInput").ap()
    dthr = nc.dram_tensor("thr", [128, NSLOT_TOT], F32, kind="ExternalInput").ap()
    dout = nc.dram_tensor("out", [TOK, E], F32, kind="ExternalOutput").ap()

    kag_in = nc.dram_tensor("kag_in", [4 * H * 128, 256], BF16, kind="Internal").ap()
    kag_out = nc.dram_tensor("kag_out", [NCORE * 4 * H * 128, 256], BF16, kind="Internal").ap()
    vag_in = nc.dram_tensor("vag_in", [TOK, E], BF16, kind="Internal").ap()
    vag_out = nc.dram_tensor("vag_out", [NCORE * TOK, E], BF16, kind="Internal").ap()
    h1_dram = nc.dram_tensor("h1_dram", [TOK, E], F32, kind="Internal").ap()
    cT_dram = nc.dram_tensor("cT_dram", [8, TOK], BF16, kind="Internal").ap()

    ident_b = nc.inline_tensor(np.eye(128, dtype=BF), "ident_b")
    ident_f = nc.inline_tensor(np.eye(128, dtype=np.float32), "ident_f")
    Rnp = np.zeros((128, 128), dtype=np.float32)
    for r in range(64):
        Rnp[64 + r, r] = -1.0
        Rnp[r, 64 + r] = 1.0
    R_c = nc.inline_tensor(Rnp.astype(BF), "Rmat")
    ones_c = nc.inline_tensor(np.ones((1, 512), dtype=BF), "ones_row")
    onesm_c = nc.inline_tensor(np.ones((128, 128), dtype=BF), "ones_mat")
    D_c = nc.inline_tensor(
        (np.arange(128, dtype=np.float32)[:, None]
         - np.arange(256, dtype=np.float32)[None, :]), "Dtab")

    st = ExitStack()
    A = st.enter_context
    b = Build(nc, st)

    dbg = {}

    def dbg_out(name, shape, dtype=F32):
        dbg[name] = nc.dram_tensor(name, shape, dtype, kind="ExternalOutput").ap()
        return dbg[name]

    id_b = A(nc.sbuf_tensor("id_b", [128, 128], BF16, side="left"))
    id_f = A(nc.sbuf_tensor("id_f", [128, 128], F32, side="left"))
    Rm = A(nc.sbuf_tensor("Rm", [128, 128], BF16, side="left"))
    onesr = A(nc.sbuf_tensor("onesr", [1, 512], BF16, side="left"))
    onesm = A(nc.sbuf_tensor("onesm", [128, 128], BF16, side="left"))
    Dt = A(nc.sbuf_tensor("Dt", [128, 256], F32, side="left"))
    thr_sb = A(nc.sbuf_tensor("thr_sb", [128, NSLOT_TOT], F32, side="left"))
    bq_sb = A(nc.sbuf_tensor("bq_sb", [1, E], BF16, side="left"))
    bk_sb = A(nc.sbuf_tensor("bk_sb", [1, E], BF16, side="left"))
    bv_sb = A(nc.sbuf_tensor("bv_sb", [1, E], BF16, side="left"))
    bot_sb = A(nc.sbuf_tensor("bot_sb", [128, 16], F32, side="left"))
    eps_sb = A(nc.sbuf_tensor("eps_sb", [128, 1], F32, side="left"))
    shf_sb = A(nc.sbuf_tensor("shf_sb", [128, 1], F32, side="left"))
    cT = A(nc.sbuf_tensor("cT", [8, TOK], BF16, side="left"))

    pQ = ExitStack()
    Qsb = pQ.enter_context(nc.sbuf_tensor("Qsb", [128, H * TOK], BF16, side="right"))

    s_ld = b.sem("s_ld")
    s_gld = b.sem("s_gld")
    s_x = b.sem("s_x")
    s_xb = b.sem("s_xb")
    s_tp = b.sem("s_tp")
    s_xT = b.sem("s_xT")
    s_w = b.sem("s_w")
    s_mm = b.sem("s_mm")
    s_ybf = b.sem("s_ybf")
    s_sh = b.sem("s_sh")
    s_t2 = b.sem("s_t2")
    s_rot = b.sem("s_rot")
    s_ve = b.sem("s_ve")
    s_st = b.sem("s_st")
    s_ccK = b.sem("s_ccK")
    s_ccV = b.sem("s_ccV")
    s_kd = b.sem("s_kd")
    s_vd = b.sem("s_vd")
    s_S = b.sem("s_S")
    s_E = b.sem("s_E")
    s_Mv = b.sem("s_Mv")
    s_Mg = b.sem("s_Mg")
    s_P = b.sem("s_P")
    s_F = b.sem("s_F")
    s_ln = b.sem("s_ln")
    s_act = b.sem("s_act")
    s_g = b.sem("s_g")
    s_mo = b.sem("s_mo")
    s_ap = b.sem("s_ap")
    s_out = b.sem("s_out")
    dsem = [b.sem(f"d{i}") for i in range(8)]

    p01 = ExitStack()
    xT = p01.enter_context(nc.sbuf_tensor("xT", [128, NEC * TOK], BF16, side="right"))
    Ksb = p01.enter_context(nc.sbuf_tensor("Ksb", [128, H * TOK], BF16, side="right"))
    cos_sb = p01.enter_context(nc.sbuf_tensor("cos_sb", [128, TOK], F32, side="right"))
    sin_sb = p01.enter_context(nc.sbuf_tensor("sin_sb", [128, TOK], F32, side="right"))

    b.dma("sync", s_ld, id_b[:], ident_b.ap())
    b.dma("sync", s_ld, id_f[:], ident_f.ap())
    b.dma("sync", s_ld, Rm[:], R_c.ap())
    b.dma("sync", s_ld, onesr[:], ones_c.ap())
    b.dma("sync", s_ld, onesm[:], onesm_c.ap())
    b.dma("sync", s_ld, Dt[:], D_c.ap())
    b.dma("sync", s_ld, thr_sb[:], dthr[:, :])
    b.dma("sync", s_ld, bot_sb[:], dbot[:, :])
    b.dma("sync", s_ld, cos_sb[:], dcos[:, :])
    b.dma("sync", s_ld, sin_sb[:], dsin[:, :])
    LD_CONST = s_ld.n
    b.dma("gpsimd", s_gld, bq_sb[:], dbq[:, :])
    b.dma("gpsimd", s_gld, bk_sb[:], dbk[:, :])
    b.dma("gpsimd", s_gld, bv_sb[:], dbv[:, :])
    GLD_BIAS = s_gld.n
    b.op("vector", "memset", eps_sb[:, 0:1], EPS, sem=s_ve)
    b.op("vector", "memset", shf_sb[:, 0:1], SHIFT, sem=s_ve)
    MEMSET_VE = s_ve.n

    with nc.sbuf_tensor("xf0", [128, 2 * E], F32, side="right") as xf0, \
         nc.sbuf_tensor("xb0", [128, 2 * E], BF16, side="right") as xb0, \
         nc.psum_tensor("pt0", [128, 1024], BF16) as pt0, \
         nc.psum_tensor("pt1", [128, 1024], BF16) as pt1:
        pts = (pt0, pt1)
        cast_t = {}
        tp_t = {}
        ev_t = {}
        xld_t = {}
        for a in (range(0) if P0_MODE == "none" else range(8)):
            par = a % 2
            if a >= 2:
                b.wait("sync", s_xb, cast_t[a - 2])
            xld_t[a] = b.dma("sync", dsem[par], xf0[:, par * E:(par + 1) * E],
                             dx[a * 128:(a + 1) * 128, :])
            b.wait("vector", dsem[par], xld_t[a])
            if a >= 2 and P0_MODE != "loadonly":
                b.wait("vector", s_tp, tp_t[(a - 2) * NEC + NEC - 1])
            cast_t[a] = b.op("vector", "tensor_copy",
                             xb0[:, par * E:(par + 1) * E],
                             xf0[:, par * E:(par + 1) * E], sem=s_xb)
            if P0_MODE == "loadonly":
                continue
            b.wait("tensor", s_xb, cast_t[a])
            if a == 0:
                b.wait("tensor", s_ld, LD_CONST)
            for ec in range(NEC):
                k = a * NEC + ec
                pt = pts[k % 2]
                col = ((k // 2) % 8) * 128
                if k >= 2:
                    b.wait("tensor", s_xT, ev_t[k - 2])
                tp_t[k] = b.op("tensor", "transpose", pt[:, col:col + 128],
                               xb0[:, par * E + ec * 128: par * E + (ec + 1) * 128],
                               id_b[:], sem=s_tp)
                b.wait("scalar", s_tp, tp_t[k])
                ev_t[k] = b.op("scalar", "copy",
                               xT[:, ec * TOK + a * 128: ec * TOK + (a + 1) * 128],
                               pt[:, col:col + 128], sem=s_xT)

    if _os.environ.get("P0_GATE", "on") == "on":
        b.gate()
    b.mark()

    def qk_proj(which, dw, bias_sb, dst):
        with nc.sbuf_tensor(f"wcol_{which}", [128, 2 * NEC * 128], BF16, side="right") as wcol, \
             nc.sbuf_tensor(f"y32_{which}", [128, 2 * 512], F32, side="right") as y32, \
             nc.sbuf_tensor(f"ysh_{which}", [128, 2 * 512], F32, side="right") as ysh, \
             nc.sbuf_tensor(f"t1_{which}", [128, 2 * 512], F32, side="right") as t1sb, \
             nc.sbuf_tensor(f"t2_{which}", [128, 2 * 512], F32, side="right") as t2sb, \
             nc.psum_tensor(f"pqk0_{which}", [128, 512], F32) as pqk0, \
             nc.psum_tensor(f"pqk1_{which}", [128, 512], F32) as pqk1:
            pqks = (pqk0, pqk1)
            wtgt = {}
            mm_t = {}
            ev_t = {}
            sh_t = {}
            t2_t = {}
            rot_t = {}
            for h in range(QK_HEADS):
                wpar = h % 2
                if h >= 2:
                    b.wait("gpsimd", s_mm, mm_t[2 * (h - 2) + 1])
                for sp in range(4):
                    wtgt[h] = b.dma(
                        "gpsimd", dsem[2 + wpar],
                        wcol[:, (wpar * NEC + sp * 4) * 128:(wpar * NEC + (sp + 1) * 4) * 128].rearrange(
                            "p (ec f) -> p ec f", ec=4),
                        dw[sp * 512:(sp + 1) * 512, h * 128:(h + 1) * 128].rearrange(
                            "(ec p) f -> p ec f", p=128))
                for tt in range(2):
                    k = 2 * h + tt
                    par = k % 2
                    b.wait("tensor", dsem[2 + wpar], wtgt[h])
                    if k >= 2:
                        b.wait("tensor", s_ybf, ev_t[k - 2])
                    for ec in range(NEC):
                        b.op("tensor", "matmul",
                             pqks[par][:, 0:512],
                             wcol[:, (wpar * NEC + ec) * 128:(wpar * NEC + ec + 1) * 128],
                             xT[:, ec * TOK + tt * 512: ec * TOK + tt * 512 + 512],
                             start=(ec == 0), stop=False)
                    mm_t[k] = b.op(
                        "tensor", "matmul", pqks[par][:, 0:512],
                        bias_sb[0:1, h * 128:(h + 1) * 128], onesr[0:1, :],
                        start=False, stop=True, sem=s_mm)
                    b.wait("scalar", s_mm, mm_t[k])
                    if k >= 2:
                        b.wait("scalar", s_t2, t2_t[k - 2])
                        b.wait("scalar", dsem[par], sh_t[k - 2])
                    ev_t[k] = b.op("scalar", "copy",
                                   y32[:, par * 512:(par + 1) * 512],
                                   pqks[par][:, 0:512], sem=s_ybf)
                    b.wait("sync", s_ybf, ev_t[k])
                    if k >= 2:
                        b.wait("sync", s_t2, t2_t[k - 2])
                    b.dma("sync", dsem[par],
                          ysh[64:128, par * 512:(par + 1) * 512],
                          y32[0:64, par * 512:(par + 1) * 512])
                    sh_t[k] = b.dma("sync", dsem[par],
                                    ysh[0:64, par * 512:(par + 1) * 512],
                                    y32[64:128, par * 512:(par + 1) * 512])
                    b.wait("vector", s_ybf, ev_t[k])
                    if k >= 2:
                        b.wait("vector", s_rot, rot_t[k - 2])
                    b.op("vector", "tensor_tensor", t1sb[:, par * 512:(par + 1) * 512],
                         y32[:, par * 512:(par + 1) * 512],
                         cos_sb[:, tt * 512:(tt + 1) * 512], ALU.mult)
                    b.wait("vector", dsem[par], sh_t[k])
                    t2_t[k] = b.op(
                        "vector", "tensor_tensor", t2sb[:, par * 512:(par + 1) * 512],
                        ysh[:, par * 512:(par + 1) * 512],
                        sin_sb[:, tt * 512:(tt + 1) * 512], ALU.mult, sem=s_t2)
                    b.wait("gpsimd", s_t2, t2_t[k])
                    rot_t[k] = b.op("gpsimd", "tensor_tensor",
                                    dst[:, h * TOK + tt * 512: h * TOK + tt * 512 + 512],
                                    t1sb[:, par * 512:(par + 1) * 512],
                                    t2sb[:, par * 512:(par + 1) * 512], ALU.add, sem=s_rot)

    b.wait("gpsimd", s_gld, GLD_BIAS)
    b.wait("tensor", s_ld, LD_CONST)
    b.wait("vector", s_ld, LD_CONST)
    qk_proj("k", dwkt, bk_sb, Ksb)
    b.mark()
    k_rot_final = s_rot.n
    b.wait("sync", s_rot, k_rot_final)
    for bb in range(4):
        for h in range(H):
            b.dma("sync", s_st, kag_in[(bb * H + h) * 128:(bb * H + h + 1) * 128, :],
                  Ksb[:, h * TOK + bb * 256: h * TOK + (bb + 1) * 256])
    kst_final = s_st.n
    b.mark()

    b.gate(engs=("tensor", "scalar", "vector", "gpsimd"))
    with nc.sbuf_tensor("wvcol", [128, 2 * NEC * 512], BF16, side="right") as wvcol, \
         nc.sbuf_tensor("vsb", [128, 2 * 512], BF16, side="right") as vsb, \
         nc.psum_tensor("pv0", [128, 512], F32) as pv0, \
         nc.psum_tensor("pv1", [128, 512], F32) as pv1:
        pvs = (pv0, pv1)
        wv_t = {}
        vmm_t = {}
        vev_t = {}
        for ft in range(4):
            wpar = ft % 2
            if ft >= 2:
                b.wait("gpsimd", s_mm, vmm_t[(ft - 2) * 8 + 7])
            for sp in range(4):
                wv_t[ft] = b.dma(
                    "gpsimd", dsem[4 + wpar],
                    wvcol[:, (wpar * NEC + sp * 4) * 512:(wpar * NEC + (sp + 1) * 4) * 512].rearrange(
                        "p (ec f) -> p ec f", ec=4),
                    dwvt[sp * 512:(sp + 1) * 512, ft * 512:(ft + 1) * 512].rearrange(
                        "(ec p) f -> p ec f", p=128))
            for a in range(8):
                k = ft * 8 + a
                par = k % 2
                b.wait("tensor", dsem[4 + wpar], wv_t[ft])
                if k >= 2:
                    b.wait("tensor", s_ve, vev_t[k - 2])
                for ec in range(NEC):
                    b.op("tensor", "matmul", pvs[par][:, 0:512],
                         xT[:, ec * TOK + a * 128: ec * TOK + (a + 1) * 128],
                         wvcol[:, (wpar * NEC + ec) * 512:(wpar * NEC + ec + 1) * 512],
                         start=(ec == 0), stop=False)
                vmm_t[k] = b.op(
                    "tensor", "matmul", pvs[par][:, 0:512],
                    onesr[0:1, 0:128], bv_sb[0:1, ft * 512:(ft + 1) * 512],
                    start=False, stop=True, sem=s_mm)
                b.wait("scalar", s_mm, vmm_t[k])
                vev_t[k] = b.op(
                    "scalar", "copy", vsb[:, par * 512:(par + 1) * 512],
                    pvs[par][:, 0:512], sem=s_ve)
                b.wait("sync", s_ve, vev_t[k])
                b.dma("sync", s_st, vag_in[a * 128:(a + 1) * 128, ft * 512:(ft + 1) * 512],
                      vsb[:, par * 512:(par + 1) * 512])
        vst_final = s_st.n

    b.mark()
    b.wait("gpsimd", s_st, vst_final)
    b.op("gpsimd", "collective_compute", "AllGather", ALU.bypass,
         [list(range(NCORE))], [kag_in[:, :]], [kag_out[:, :]],
         sem=s_ccK, sem_bare=True)
    b.op("gpsimd", "collective_compute", "AllGather", ALU.bypass,
         [list(range(NCORE))], [vag_in[:, :]], [vag_out[:, :]],
         sem=s_ccV, sem_bare=True)
    b.mark()

    b.gate(engs=("tensor", "scalar", "vector", "gpsimd"), exclude=(s_ccK, s_ccV))
    qk_proj("q", dwqt, bq_sb, Qsb)

    if "p1" in DEBUG_TAPS:
        dq = dbg_out("dbg_Q", [128, H * TOK], BF16)
        dk = dbg_out("dbg_K", [4 * H * 128, 256], BF16)
        dv = dbg_out("dbg_V", [TOK, E], BF16)
        b.gate(engs=("sync",), exclude=(s_ccK, s_ccV))
        b.dma("sync", s_st, dq[:, :], Qsb[:])
        b.dma("sync", s_st, dk[:, :], kag_in[:, :])
        b.dma("sync", s_st, dv[:, :], vag_in[:, :])

    p01.close()
    p23 = ExitStack()
    attnT = p23.enter_context(
        nc.sbuf_tensor("attnT", [128, NEC * TOK], BF16, side="left"))
    b.gate()
    b.wait("sync", s_ccK, 1)
    b.wait("sync", s_ccV, 1)
    b.wait("tensor", s_ccK, 1)
    b.wait("tensor", s_ccV, 1)
    b.mark()

    with nc.sbuf_tensor("maskb", [128, 64 * 256], BF16, side="right") as maskb, \
         nc.sbuf_tensor("kbuf", [128, 4 * 1024], BF16, side="right") as kbuf, \
         nc.sbuf_tensor("vbuf", [128, 4 * 512], BF16, side="right") as vbuf, \
         nc.sbuf_tensor("ptb", [128, 2 * 1024], BF16, side="right") as ptb, \
         nc.sbuf_tensor("recip", [128, 1024], F32, side="right") as recip, \
         nc.sbuf_tensor("den_sb", [128, 1024], F32, side="right") as den_sb, \
         nc.sbuf_tensor("acc_sb", [128, 1024], F32, side="right") as acc_sb, \
         nc.psum_tensor("sregA", [128, 1024], F32) as sregA, \
         nc.psum_tensor("sregB", [128, 1024], F32) as sregB, \
         nc.psum_tensor("pvacc", [128, 1024], F32) as pvacc, \
         nc.psum_tensor("pden", [128, 1024], F32) as pden:
        sregs = (sregA, sregB)

        Sg, Eg, Pg, Vg, Kg, Mgt, Mge = [], [], [], [], [], [], []
        f_t = {}
        seq = [(j, g) for j in range(4) for g in range(4)]
        glob = 0
        globp = 0

        for si, (j, g) in enumerate(seq):
            n = NSLOT[j]
            for i in range(n):
                G = glob + i
                par = G % 2
                half = i % 2
                gb = i // 2
                r, jl = gb % 8, gb // 8
                ksl = (globp + gb) % 4
                vsl = G % 4
                eng = "vector" if G % 2 == 0 else "gpsimd"
                sm = s_Mv if G % 2 == 0 else s_Mg

                if i % 2 == 0:
                    P = globp + gb
                    if P >= 4:
                        b.wait("sync", s_S, Sg[2 * (P - 4) + 1])
                    Kg.append(b.dma(
                        "sync", dsem[P % 4],
                        kbuf[:, ksl * 1024:(ksl + 1) * 1024].rearrange(
                            "d (h k) -> d h k", h=4),
                        kag_out[(r * 4 + jl) * H * 128 + 4 * g * 128:
                                (r * 4 + jl) * H * 128 + 4 * (g + 1) * 128, :].rearrange(
                            "(h d) k -> d h k", d=128)))
                if G >= 4:
                    b.wait("sync", s_P, Pg[G - 4])
                Vg.append(b.dma(
                    "sync", dsem[4 + G % 4], vbuf[:, vsl * 512:(vsl + 1) * 512],
                    vag_out[r * TOK + jl * 256 + half * 128:
                            r * TOK + jl * 256 + half * 128 + 128,
                            4 * g * 128:4 * (g + 1) * 128]))

                b.wait("tensor", dsem[(globp + gb) % 4], Kg[globp + gb])
                if G >= 2:
                    b.wait("tensor", s_E, Eg[G - 2])
                for u in range(4):
                    kw = {}
                    if u == 3:
                        kw["sem"] = s_S
                    b.op("tensor", "matmul",
                         sregs[par][:, u * 256:(u + 1) * 256],
                         kbuf[:, ksl * 1024 + u * 256 + half * 128:
                              ksl * 1024 + u * 256 + half * 128 + 128],
                         Qsb[:, (4 * g + u) * TOK + 256 * j:
                             (4 * g + u) * TOK + 256 * (j + 1)],
                         start=True, stop=True, **kw)
                Sg.append(s_S.n)

                b.wait("scalar", s_S, Sg[G])
                if G >= 2:
                    b.wait("scalar", s_P, Pg[G - 2])
                if G == 0:
                    b.wait("scalar", s_ve, MEMSET_VE)
                Eg.append(b.op(
                    "scalar", "activation", ptb[:, par * 1024:(par + 1) * 1024],
                    sregs[par][:, 0:1024], AF.Exp,
                    bias=shf_sb[:, 0:1], scale=SCALE, sem=s_E))

                if g == 0:
                    b.wait(eng, s_ld, LD_CONST)
                    b.op(eng, "tensor_scalar", maskb[:, i * 256:(i + 1) * 256],
                         Dt[:], thr_sb[:, SLOT0[j] + i:SLOT0[j] + i + 1], None,
                         ALU.is_le)
                b.wait(eng, s_E, Eg[G])
                for u in range(4):
                    kw = {}
                    if u == 3:
                        kw["sem"] = sm
                    b.op(eng, "tensor_tensor",
                         ptb[:, par * 1024 + u * 256: par * 1024 + (u + 1) * 256],
                         ptb[:, par * 1024 + u * 256: par * 1024 + (u + 1) * 256],
                         maskb[:, i * 256:(i + 1) * 256], ALU.mult, **kw)
                if G % 2 == 0:
                    Mgt.append(sm.n)
                    Mge.append(None)
                else:
                    Mgt.append(sm.n)
                    Mge.append(None)

                b.wait("tensor", sm, Mgt[G])
                b.wait("tensor", dsem[4 + G % 4], Vg[G])
                if i == 0 and si > 0:
                    b.wait("tensor", s_F, f_t[seq[si - 1]])
                for u in range(4):
                    b.op("tensor", "matmul", pvacc[:, u * 256:(u + 1) * 256],
                         vbuf[:, vsl * 512 + u * 128: vsl * 512 + (u + 1) * 128],
                         ptb[:, par * 1024 + u * 256: par * 1024 + (u + 1) * 256],
                         start=(i == 0 and u in (0, 2)), stop=(i == n - 1))
                b.op("tensor", "matmul", pden[:, 0:512], onesm[:],
                     ptb[:, par * 1024: par * 1024 + 512],
                     start=(i == 0), stop=(i == n - 1))
                Pg.append(b.op(
                    "tensor", "matmul", pden[:, 512:1024], onesm[:],
                    ptb[:, par * 1024 + 512: par * 1024 + 1024],
                    start=(i == 0), stop=(i == n - 1), sem=s_P))

            b.wait("scalar", s_P, Pg[glob + n - 1])
            if si > 0:
                b.wait("scalar", s_F, f_t[seq[si - 1]])
            b.op("scalar", "copy", den_sb[:], pden[:])
            ev_fin = b.op("scalar", "copy", acc_sb[:], pvacc[:], sem=s_ybf)
            b.wait("vector", s_ybf, ev_fin)
            b.op("vector", "reciprocal", recip[:], den_sb[:])
            for u in range(4):
                kw = {}
                if u == 3:
                    kw["sem"] = s_F
                b.op("vector", "tensor_tensor",
                     attnT[:, (4 * g + u) * TOK + 256 * j:
                           (4 * g + u) * TOK + 256 * (j + 1)],
                     acc_sb[:, u * 256:(u + 1) * 256],
                     recip[:, u * 256:(u + 1) * 256], ALU.mult, **kw)
            f_t[(j, g)] = s_F.n
            glob += n
            globp += n // 2

    if "p2" in DEBUG_TAPS:
        d = dbg_out("dbg_attnT", [128, NEC * TOK], BF16)
        b.gate(engs=("sync",))
        b.dma("sync", s_st, d[:, :], attnT[:])

    pQ.close()
    b.gate()
    b.mark()

    h1T = A(nc.sbuf_tensor("h1T", [128, NEC * TOK], BF16, side="right"))

    p3a = ExitStack()
    E3 = p3a.enter_context
    po3a = E3(nc.psum_tensor("po3a", [128, 512], F32))
    po3b = E3(nc.psum_tensor("po3b", [128, 512], F32))
    po3s = (po3a, po3b)
    pt3a = E3(nc.psum_tensor("pt3a", [128, 512], F32))
    pt3b = E3(nc.psum_tensor("pt3b", [128, 512], F32))
    pt3s = (pt3a, pt3b)
    ptb3 = E3(nc.psum_tensor("ptb3", [128, 1024], BF16))
    ptb3b = E3(nc.psum_tensor("ptb3b", [128, 1024], BF16))
    otm = E3(nc.sbuf_tensor("otm", [128, 8 * E], F32, side="right"))
    p3b = ExitStack()
    wocol = p3b.enter_context(
        nc.sbuf_tensor("wocol", [128, 2 * NEC * 128], BF16, side="right"))
    osb = p3b.enter_context(
        nc.sbuf_tensor("osb", [128, 2 * 512], F32, side="right"))
    if 1:
        ptb3s = (ptb3, ptb3b)

        wo_t = {}
        omm_t = {}
        oev_t = {}
        obias_t = {}
        otm_t = {}
        for fc in range(NEC):
            wpar = fc % 2
            if fc >= 2:
                b.wait("gpsimd", s_mm, omm_t[2 * (fc - 2) + 1])
            for sp in range(4):
                wo_t[fc] = b.dma(
                    "gpsimd", dsem[2 + wpar],
                    wocol[:, (wpar * NEC + sp * 4) * 128:(wpar * NEC + (sp + 1) * 4) * 128].rearrange(
                        "p (ec f) -> p ec f", ec=4),
                    dwot[sp * 512:(sp + 1) * 512, fc * 128:(fc + 1) * 128].rearrange(
                        "(ec p) f -> p ec f", p=128))
            for tt in range(2):
                k = 2 * fc + tt
                par = k % 2
                b.wait("tensor", dsem[2 + wpar], wo_t[fc])
                if k >= 2:
                    b.wait("tensor", s_ybf, oev_t[k - 2])
                for ec in range(NEC):
                    kw = dict(start=(ec == 0), stop=(ec == NEC - 1))
                    if ec == NEC - 1:
                        kw["sem"] = s_mm
                    b.op("tensor", "matmul", po3s[par][:, 0:512],
                         wocol[:, (wpar * NEC + ec) * 128:(wpar * NEC + ec + 1) * 128],
                         attnT[:, ec * TOK + tt * 512: ec * TOK + tt * 512 + 512],
                         **kw)
                omm_t[k] = s_mm.n
                b.wait("scalar", s_mm, omm_t[k])
                if k >= 2:
                    b.wait("scalar", s_ve, obias_t[k - 2])
                    b.wait("scalar", s_tp, tp3_last(k - 2, otm_t))
                oac_t = b.op(
                    "scalar", "copy", osb[:, par * 512:(par + 1) * 512],
                    po3s[par][:, 0:512], sem=s_ybf)
                oev_t[k] = oac_t
                b.wait("vector", s_ybf, oac_t)
                if k >= 2:
                    b.wait("vector", s_tp, tp3_last(k - 2, otm_t))
                obias_t[k] = b.op(
                    "vector", "tensor_scalar", osb[:, par * 512:(par + 1) * 512],
                    osb[:, par * 512:(par + 1) * 512], bot_sb[:, fc:fc + 1], None,
                    ALU.add, sem=s_ve)
                b.wait("tensor", s_ve, obias_t[k])
                for q in range(4):
                    a = tt * 4 + q
                    kk = 8 * fc + 4 * tt + q
                    col3 = ((kk // 2) % 4) * 128
                    if kk >= 2:
                        b.wait("tensor", s_xT, otm_t[kk - 2][1])
                    t = b.op("tensor", "transpose", pt3s[kk % 2][:, col3:col3 + 128],
                             osb[:, par * 512 + q * 128: par * 512 + (q + 1) * 128],
                             id_f[:], sem=s_tp)
                    b.wait("scalar", s_tp, t)
                    otm_t[kk] = (t, b.op(
                        "scalar", "copy",
                        otm[:, a * E + fc * 128: a * E + (fc + 1) * 128],
                        pt3s[kk % 2][:, col3:col3 + 128], sem=s_xT))

        if "p3o" in DEBUG_TAPS or "p3q" in DEBUG_TAPS:
            do3 = dbg_out("dbg_otm", [128, 8 * E])
            b.gate(engs=("sync",))
            b.dma("sync", s_st, do3[:, :], otm[:])
            if "p3o" in DEBUG_TAPS:
                b.wait("vector", s_st, s_st.n)
        p3b.close()
        p23.close()
        xf3 = E3(nc.sbuf_tensor("xf3", [128, 2 * E], F32, side="right"))
        h1f = E3(nc.sbuf_tensor("h1f", [128, 2 * E], F32, side="right"))
        h1b = E3(nc.sbuf_tensor("h1b", [128, 2 * E], BF16, side="right"))
        sq3 = E3(nc.sbuf_tensor("sq3", [128, E], F32, side="right"))
        ln3 = E3(nc.sbuf_tensor("ln3", [128, 8], F32, side="right"))
        glog = E3(nc.sbuf_tensor("glog", [8, TOK], F32, side="right"))
        gtm = E3(nc.sbuf_tensor("gtm", [128, 64], F32, side="right"))
        top2 = E3(nc.sbuf_tensor("top2", [128, 52], F32, side="right"))
        ctmb = E3(nc.sbuf_tensor("ctmb", [128, 8], BF16, side="right"))
        gw_sb = E3(nc.sbuf_tensor("gw_sb", [128, 128], BF16, side="right"))
        ln1_t = {}
        h1st_t = {}
        h1T_t = {}
        for a in range(8):
            par = a % 2
            if a < 2:
                b.wait("sync", s_xT, otm_t[8 * NEC - 1][1])
            if a >= 2:
                b.wait("sync", s_ln, ln1_t[a - 2])
            xld = b.dma("sync", dsem[par], xf3[:, par * E:(par + 1) * E],
                        dx[a * 128:(a + 1) * 128, :])
            b.wait("vector", dsem[par], xld)
            b.wait("vector", s_xT,
                   max(otm_t[8 * fc_ + 4 * (a // 4) + (a % 4)][1] for fc_ in range(NEC)))
            st_t = b.op("vector", "scalar_tensor_tensor", h1f[:, par * E:(par + 1) * E],
                        xf3[:, par * E:(par + 1) * E], 1.0, otm[:, a * E:(a + 1) * E],
                        ALU.mult, ALU.add, sem=s_ln)
            b.wait("vector", s_ln, st_t)
            rd_t = b.op("vector", "reduce_sum", ln3[:, 0:1], h1f[:, par * E:(par + 1) * E],
                        AX.X, sem=s_ln)
            if P3_LN == "raw":
                h1w_t = st_t
                cast_t = b.op("vector", "tensor_copy", h1b[:, par * E:(par + 1) * E],
                              h1f[:, par * E:(par + 1) * E], sem=s_ln)
                ln1_t[a] = cast_t
                b.wait("sync", s_ln, h1w_t)
                h1st_t[a] = b.dma("sync", dsem[6 + par], h1_dram[a * 128:(a + 1) * 128, :],
                                  h1f[:, par * E:(par + 1) * E])
                b.wait("tensor", s_ln, cast_t)
                for ec in range(NEC):
                    kk = a * NEC + ec
                    ptq = ptb3s[kk % 2]
                    colq = ((kk // 2) % 8) * 128
                    if kk >= 2:
                        b.wait("tensor", s_xT, h1T_t[kk - 2][1])
                    t = b.op("tensor", "transpose", ptq[:, colq:colq + 128],
                             h1b[:, par * E + ec * 128: par * E + (ec + 1) * 128],
                             id_b[:], sem=s_tp)
                    b.wait("scalar", s_tp, t)
                    h1T_t[kk] = (t, b.op(
                        "scalar", "copy",
                        h1T[:, ec * TOK + a * 128: ec * TOK + (a + 1) * 128],
                        ptq[:, colq:colq + 128], sem=s_xT))
                continue
            b.wait("vector", s_ln, rd_t)
            mn_t = b.op("vector", "tensor_scalar", ln3[:, 1:2], ln3[:, 0:1], 1.0 / E,
                        None, ALU.mult, sem=s_ln)
            b.wait("vector", s_ln, mn_t)
            b.op("vector", "tensor_scalar", otm[:, a * E:(a + 1) * E],
                 h1f[:, par * E:(par + 1) * E], ln3[:, 1:2], None, ALU.subtract)
            b.op("vector", "scalar_tensor_tensor", sq3[:],
                 otm[:, a * E:(a + 1) * E], 1.0, otm[:, a * E:(a + 1) * E],
                 ALU.mult, ALU.mult)
            sq_t = b.op("vector", "reduce_sum", ln3[:, 2:3], sq3[:], AX.X, sem=s_ln)
            b.wait("scalar", s_ln, sq_t)
            if a == 0:
                b.wait("scalar", s_ve, MEMSET_VE)
            lnv_t = b.op("scalar", "activation", ln3[:, 3:4], ln3[:, 2:3], AF.Ln,
                         bias=eps_sb[:, 0:1], scale=1.0 / E, sem=s_act)
            b.wait("scalar", s_act, lnv_t)
            rs_t = b.op("scalar", "activation", ln3[:, 4:5], ln3[:, 3:4], AF.Exp,
                        scale=-0.5, sem=s_act)
            b.wait("vector", s_act, rs_t)
            if a >= 2:
                b.wait("vector", dsem[6 + par], h1st_t[a - 2])
                b.wait("vector", s_tp, h1T_t[(a - 2) * NEC + NEC - 1][0])
            h1w_t = b.op("vector", "tensor_scalar", h1f[:, par * E:(par + 1) * E],
                         otm[:, a * E:(a + 1) * E], ln3[:, 4:5], None, ALU.mult,
                         sem=s_ln)
            cast_t = b.op("vector", "tensor_copy", h1b[:, par * E:(par + 1) * E],
                          h1f[:, par * E:(par + 1) * E], sem=s_ln)
            ln1_t[a] = cast_t
            b.wait("sync", s_ln, h1w_t)
            h1st_t[a] = b.dma("sync", dsem[6 + par], h1_dram[a * 128:(a + 1) * 128, :],
                              h1f[:, par * E:(par + 1) * E])
            b.wait("tensor", s_ln, cast_t)
            for ec in range(NEC):
                kk = a * NEC + ec
                ptq = ptb3s[kk % 2]
                colq = ((kk // 2) % 8) * 128
                if kk >= 2:
                    b.wait("tensor", s_xT, h1T_t[kk - 2][1])
                t = b.op("tensor", "transpose", ptq[:, colq:colq + 128],
                         h1b[:, par * E + ec * 128: par * E + (ec + 1) * 128],
                         id_b[:], sem=s_tp)
                b.wait("scalar", s_tp, t)
                h1T_t[kk] = (t, b.op(
                    "scalar", "copy",
                    h1T[:, ec * TOK + a * 128: ec * TOK + (a + 1) * 128],
                    ptq[:, colq:colq + 128], sem=s_xT))

        b.dma("gpsimd", s_gld, gw_sb[:], dgwt[:, :])
        b.wait("tensor", s_gld, s_gld.n)
        b.wait("tensor", s_xT, h1T_t[8 * NEC - 1][1])
        b.wait("tensor", s_ybf, oev_t[2 * NEC - 1])
        for tt in range(2):
            for ec in range(NEC):
                kw = dict(start=(ec == 0), stop=(ec == NEC - 1))
                if ec == NEC - 1:
                    kw["sem"] = s_mm
                b.op("tensor", "matmul", po3s[tt][0:8, 0:512],
                     gw_sb[:, ec * 8:(ec + 1) * 8],
                     h1T[:, ec * TOK + tt * 512: ec * TOK + tt * 512 + 512], **kw)
        gmm_t = s_mm.n
        b.wait("scalar", s_mm, gmm_t)
        b.op("scalar", "copy", glog[0:8, 0:512], po3s[0][0:8, 0:512])
        ge_t = b.op("scalar", "copy", glog[0:8, 512:1024], po3s[1][0:8, 0:512], sem=s_ve)
        b.wait("tensor", s_ve, ge_t)
        b.wait("tensor", s_xT, otm_t[8 * NEC - 1][1])
        gev = {}
        for a in range(8):
            if a >= 2:
                b.wait("tensor", s_ve, gev[a - 2])
            t = b.op("tensor", "transpose", pt3s[a % 2][:, 0:8],
                     glog[0:8, a * 128:(a + 1) * 128], id_f[0:8, 0:8], sem=s_tp)
            b.wait("scalar", s_tp, t)
            gev[a] = b.op("scalar", "copy", gtm[:, a * 8:(a + 1) * 8],
                          pt3s[a % 2][:, 0:8], sem=s_ve)
        ct_ev = {}
        for a in range(8):
            ga = gtm[:, a * 8:(a + 1) * 8]
            ism = top2[:, 0:8]
            msk2 = top2[:, 8:16]
            iss = top2[:, 16:24]
            sel = top2[:, 24:32]
            ex = top2[:, 32:40]
            cu = top2[:, 40:48]
            m1n = top2[:, 48:49]
            m2n = top2[:, 49:50]
            den = top2[:, 50:51]
            rec = top2[:, 51:52]
            b.wait("vector", s_ve, gev[a])
            def _edge(t):
                b.wait("vector", s_g, t)
            r1 = b.op("vector", "reduce_max", m1n, ga, AX.X, negate=True, sem=s_g)
            _edge(r1)
            _edge(b.op("vector", "tensor_scalar", ism, ga, m1n, 0.0, ALU.add,
                       ALU.is_ge, sem=s_g))
            _edge(b.op("vector", "scalar_tensor_tensor", msk2, ism, -1e30, ga,
                       ALU.mult, ALU.add, sem=s_g))
            _edge(b.op("vector", "reduce_max", m2n, msk2, AX.X, negate=True, sem=s_g))
            _edge(b.op("vector", "tensor_scalar", iss, msk2, m2n, 0.0, ALU.add,
                       ALU.is_ge, sem=s_g))
            sel_t = b.op("vector", "tensor_tensor", sel, ism, iss, ALU.add, sem=s_g)
            _edge(sel_t)
            b.wait("scalar", s_g, r1)
            ex_t = b.op("scalar", "activation", ex, ga, AF.Exp, bias=m1n, sem=s_act)
            b.wait("vector", s_act, ex_t)
            _edge(b.op("vector", "tensor_tensor", cu, sel, ex, ALU.mult, sem=s_g))
            _edge(b.op("vector", "reduce_sum", den, cu, AX.X, sem=s_g))
            _edge(b.op("vector", "reciprocal", rec, den, sem=s_g))
            cb_t = b.op("vector", "tensor_scalar", ctmb[:, 0:8], cu, rec, None,
                        ALU.mult, sem=s_g)
            b.wait("tensor", s_g, cb_t)
            ptc = ptb3s[a % 2]
            if a >= 2:
                b.wait("tensor", s_ve, ct_ev[a - 2])
            ct = b.op("tensor", "transpose", ptc[0:8, 0:128],
                      ctmb[:, 0:8], id_b[:], sem=s_tp)
            b.wait("scalar", s_tp, ct)
            ct_ev[a] = b.op("scalar", "copy", cT[0:8, a * 128:(a + 1) * 128],
                            ptc[0:8, 0:128], sem=s_ve)

        b.wait("sync", s_ve, ct_ev[7])
        b.dma("sync", s_st, cT_dram[:, :], cT[0:8, :])

    p3a.close()
    b.gate()
    b.mark()

    p4s = ExitStack()
    E4 = p4s.enter_context
    w1col = E4(nc.sbuf_tensor("w1col", [128, 2 * NEC * 128], BF16, side="right"))
    w2col = E4(nc.sbuf_tensor("w2col", [128, 2 * NIC * 128], BF16, side="right"))
    silu_sb = E4(nc.sbuf_tensor("silu_sb", [128, 4 * 512], BF16, side="right"))
    tmpb = E4(nc.sbuf_tensor("tmpb", [128, 2 * 512], BF16, side="right"))
    ub_sb = E4(nc.sbuf_tensor("ub_sb", [128, 2 * 512], BF16, side="right"))
    mot_sb = E4(nc.sbuf_tensor("mot_sb", [128, 1024], F32, side="right"))
    crep = E4(nc.sbuf_tensor("crep", [128, 1024], BF16, side="right"))
    crow = E4(nc.sbuf_tensor("crow", [1, 2 * TOK], BF16, side="right"))
    actp = E4(nc.sbuf_tensor("actp", [128, NOC * 512], BF16, side="right"))
    mo_sb = E4(nc.sbuf_tensor("mo_sb", [128, NEC * TOK], F32, side="right"))
    h1f4 = E4(nc.sbuf_tensor("h1f4", [128, E], F32, side="right"))
    motm = E4(nc.sbuf_tensor("motm", [128, E], F32, side="right"))
    outb = E4(nc.sbuf_tensor("outb", [128, E], F32, side="right"))
    sq4 = E4(nc.sbuf_tensor("sq4", [128, E], F32, side="right"))
    ln4 = E4(nc.sbuf_tensor("ln4", [128, 8], F32, side="right"))
    pgu0 = E4(nc.psum_tensor("pgu0", [128, 1024], F32))
    pgu1 = E4(nc.psum_tensor("pgu1", [128, 1024], F32))
    pmo0 = E4(nc.psum_tensor("pmo0", [128, 1024], F32))
    pmo1 = E4(nc.psum_tensor("pmo1", [128, 1024], F32))
    if 1:
        pgus = (pgu0, pgu1)
        pmos = (pmo0, pmo1)

        gumm = {}
        crm_t = {}
        silu_t = {}
        tmp_t = {}
        ap_t = {}
        w2mm = {}
        moadd = {}
        crev = {}
        crl_t = {}
        for e in range(NE):
            epar = e % 2
            if e >= 2:
                b.wait("sync", s_mm, crm_t[e - 2])
            crl_t[e] = b.dma("sync", dsem[6 + epar], crow[0:1, epar * TOK:(epar + 1) * TOK],
                             cT_dram[e:e + 1, :])
            if e >= 1:
                b.wait("tensor", s_mo, moadd[(e - 1, 14)])
            b.wait("tensor", dsem[6 + epar], crl_t[e])
            for tt in range(2):
                kw = dict(start=True, stop=True)
                if tt == 1:
                    kw["sem"] = s_mm
                b.op("tensor", "matmul", pmos[0][:, tt * 512:(tt + 1) * 512],
                     onesr[0:1, 0:128],
                     crow[0:1, epar * TOK + tt * 512: epar * TOK + (tt + 1) * 512], **kw)
            crm = s_mm.n
            crm_t[e] = crm
            b.wait("scalar", s_mm, crm)
            crev[e] = b.op("scalar", "copy", crep[:], pmos[0][:, 0:1024],
                           sem=s_ve)

            for p in range(NOC):
                ic = p // 2
                is_g = (p % 2 == 0)
                oc = ic if is_g else NIC + ic
                gp = e * NOC + p
                wpar = p % 2
                if gp >= 2:
                    b.wait("gpsimd", s_mm, gumm[gp - 2])
                for sp in range(4):
                    wt = b.dma(
                        "gpsimd", dsem[2 + wpar],
                        w1col[:, (wpar * NEC + sp * 4) * 128:(wpar * NEC + (sp + 1) * 4) * 128].rearrange(
                            "p (ec f) -> p ec f", ec=4),
                        dw1t[e, sp * 512:(sp + 1) * 512, oc * 128:(oc + 1) * 128].rearrange(
                            "(ec p) f -> p ec f", p=128))
                b.wait("tensor", dsem[2 + wpar], wt)
                if gp >= 2:
                    pic, pe = (ic - 1, e) if ic >= 1 else (NIC - 1, e - 1)
                    b.wait("tensor", s_ybf, silu_t[(pe, pic, 1)])
                for tt in range(2):
                    for ec in range(NEC):
                        kw = dict(start=(ec == 0), stop=(ec == NEC - 1))
                        if tt == 1 and ec == NEC - 1:
                            kw["sem"] = s_mm
                        b.op("tensor", "matmul", pgus[wpar][:, tt * 512:(tt + 1) * 512],
                             w1col[:, (wpar * NEC + ec) * 128:(wpar * NEC + ec + 1) * 128],
                             h1T[:, ec * TOK + tt * 512: ec * TOK + tt * 512 + 512],
                             **kw)
                gumm[gp] = s_mm.n
                if is_g:
                    b.wait("scalar", s_mm, gumm[gp])
                    if ic >= 2 or e >= 1:
                        pic, pe = (ic - 2, e) if ic >= 2 else (NIC - 2 + ic, e - 1)
                        b.wait("scalar", s_ap, ap_t[(pe, pic, 1)])
                    for tt in range(2):
                        silu_t[(e, ic, tt)] = b.op(
                            "scalar", "activation",
                            silu_sb[:, ((ic % 2) * 2 + tt) * 512:((ic % 2) * 2 + tt + 1) * 512],
                            pgus[wpar][:, tt * 512:(tt + 1) * 512], AF.Silu,
                            sem=s_ybf)
                else:
                    b.wait("scalar", s_mm, gumm[gp])
                    if ic >= 1 or e >= 1:
                        pic2, pe2 = (ic - 1, e) if ic >= 1 else (NIC - 1, e - 1)
                        b.wait("scalar", s_ve, tmp_t[(pe2, pic2, 1)])
                    uev = {}
                    for tt in range(2):
                        uev[tt] = b.op(
                            "scalar", "copy", ub_sb[:, tt * 512:(tt + 1) * 512],
                            pgus[wpar][:, tt * 512:(tt + 1) * 512], sem=s_ybf)
                    b.wait("vector", s_ve, crev[e])
                    for tt in range(2):
                        b.wait("vector", s_ybf, uev[tt])
                        if ic >= 1 or e >= 1:
                            pic2, pe2 = (ic - 1, e) if ic >= 1 else (NIC - 1, e - 1)
                            b.wait("vector", s_ap, ap_t[(pe2, pic2, tt)])
                        tmp_t[(e, ic, tt)] = b.op(
                            "vector", "tensor_tensor", tmpb[:, tt * 512:(tt + 1) * 512],
                            ub_sb[:, tt * 512:(tt + 1) * 512],
                            crep[:, tt * 512:(tt + 1) * 512], ALU.mult, sem=s_ve)
                    for tt in range(2):
                        b.wait("gpsimd", s_ve, tmp_t[(e, ic, tt)])
                        b.wait("gpsimd", s_ybf, silu_t[(e, ic, tt)])
                        if e >= 1:
                            b.wait("gpsimd", s_mm, w2mm[(e - 1, 15)])
                        ap_t[(e, ic, tt)] = b.op(
                            "gpsimd", "tensor_tensor",
                            actp[:, (tt * NIC + ic) * 512:(tt * NIC + ic + 1) * 512],
                            tmpb[:, tt * 512:(tt + 1) * 512],
                            silu_sb[:, ((ic % 2) * 2 + tt) * 512:((ic % 2) * 2 + tt + 1) * 512],
                            ALU.mult, sem=s_ap)

            for hc in range(NEC):
                wpar2 = hc % 2
                if (e, hc - 2) in w2mm:
                    b.wait("gpsimd", s_mm, w2mm[(e, hc - 2)])
                elif e >= 1:
                    b.wait("gpsimd", s_mm, w2mm[(e - 1, 14 + hc)])
                for sp in range(2):
                    nic_sp = min((sp + 1) * 6, NIC) - sp * 6
                    wt2 = b.dma(
                        "gpsimd", dsem[4 + wpar2],
                        w2col[:, (wpar2 * NIC + sp * 6) * 128:
                              (wpar2 * NIC + min((sp + 1) * 6, NIC)) * 128].rearrange(
                            "p (ic f) -> p ic f", ic=nic_sp),
                        dw2t[e, sp * 768:min((sp + 1) * 768, I1),
                             hc * 128:(hc + 1) * 128].rearrange(
                            "(ic p) f -> p ic f", p=128))
                b.wait("tensor", dsem[4 + wpar2], wt2)
                if hc == 0:
                    b.wait("tensor", s_ap, ap_t[(e, NIC - 1, 1)])
                    b.wait("tensor", s_ve, crev[e])
                if (e, hc - 2) in moadd:
                    b.wait("tensor", s_mo, moadd[(e, hc - 2)])
                elif e >= 1 and hc >= 1:
                    b.wait("tensor", s_mo, moadd[(e - 1, 14 + hc - 1)])
                for tt in range(2):
                    for ic in range(NIC):
                        kw = dict(start=(ic == 0), stop=(ic == NIC - 1))
                        if tt == 1 and ic == NIC - 1:
                            kw["sem"] = s_mm
                        b.op("tensor", "matmul",
                             pmos[wpar2][:, tt * 512:(tt + 1) * 512],
                             w2col[:, (wpar2 * NIC + ic) * 128:(wpar2 * NIC + ic + 1) * 128],
                             actp[:, (tt * NIC + ic) * 512:(tt * NIC + ic + 1) * 512],
                             **kw)
                w2mm[(e, hc)] = s_mm.n
                b.wait("scalar", s_mm, w2mm[(e, hc)])
                if (e, hc) != (0, 0):
                    b.wait("scalar", s_mo, s_mo.n)
                mev = b.op("scalar", "copy", mot_sb[:, 0:1024],
                           pmos[wpar2][:, 0:1024], sem=s_ybf)
                b.wait("vector", s_ybf, mev)
                for tt in range(2):
                    kw = {}
                    if tt == 1:
                        kw["sem"] = s_mo
                    dstm = mo_sb[:, hc * TOK + tt * 512: hc * TOK + tt * 512 + 512]
                    if e == 0:
                        b.op("vector", "tensor_copy", dstm,
                             mot_sb[:, tt * 512:(tt + 1) * 512], **kw)
                    else:
                        b.op("vector", "tensor_tensor", dstm, dstm,
                             mot_sb[:, tt * 512:(tt + 1) * 512], ALU.add, **kw)
                moadd[(e, hc)] = s_mo.n

        ln2_t = {}
        mev_t = {}
        ost_t = {}
        for a in range(8):
            if a >= 1:
                b.wait("sync", s_ln, ln2_t[a - 1])
            h1ld = b.dma("sync", dsem[a % 2], h1f4[:, 0:E],
                         h1_dram[a * 128:(a + 1) * 128, :])
            if a == 0:
                b.wait("tensor", s_mo, moadd[(NE - 1, NEC - 1)])
                b.wait("tensor", s_ybf, s_ybf.n)
            for hc in range(NEC):
                kk = a * NEC + hc
                pg = pgus[kk % 2]
                col = ((kk // 2) % 2) * 512
                if kk >= 4:
                    b.wait("tensor", s_xT, mev_t[kk - 4])
                t = b.op("tensor", "transpose", pg[:, col:col + 128],
                         mo_sb[:, hc * TOK + a * 128: hc * TOK + (a + 1) * 128],
                         id_f[:], sem=s_tp)
                b.wait("scalar", s_tp, t)
                if a >= 1 and hc == 0:
                    b.wait("scalar", s_ln, ln2_t[a - 1])
                mev_t[kk] = b.op(
                    "scalar", "copy",
                    motm[:, hc * 128:(hc + 1) * 128],
                    pg[:, col:col + 128], sem=s_xT)
            b.wait("vector", dsem[a % 2], h1ld)
            b.wait("vector", s_xT, mev_t[a * NEC + NEC - 1])
            if a >= 1:
                b.wait("vector", s_out, ost_t[a - 1])
            st_t = b.op("vector", "scalar_tensor_tensor", outb[:, 0:E],
                        h1f4[:, 0:E], 1.0, motm[:, 0:E],
                        ALU.mult, ALU.add, sem=s_ln)
            b.wait("vector", s_ln, st_t)
            rd4_t = b.op("vector", "reduce_sum", ln4[:, 0:1], outb[:, 0:E], AX.X,
                         sem=s_ln)
            b.wait("vector", s_ln, rd4_t)
            mn4_t = b.op("vector", "tensor_scalar", ln4[:, 1:2], ln4[:, 0:1], 1.0 / E,
                         None, ALU.mult, sem=s_ln)
            b.wait("vector", s_ln, mn4_t)
            b.op("vector", "tensor_scalar", motm[:, 0:E],
                 outb[:, 0:E], ln4[:, 1:2], None, ALU.subtract)
            b.op("vector", "scalar_tensor_tensor", sq4[:, 0:E],
                 motm[:, 0:E], 1.0, motm[:, 0:E],
                 ALU.mult, ALU.mult)
            sq_t = b.op("vector", "reduce_sum", ln4[:, 2:3], sq4[:, 0:E], AX.X, sem=s_ln)
            b.wait("scalar", s_ln, sq_t)
            lnv4_t = b.op("scalar", "activation", ln4[:, 3:4], ln4[:, 2:3], AF.Ln,
                          bias=eps_sb[:, 0:1], scale=1.0 / E, sem=s_act)
            b.wait("scalar", s_act, lnv4_t)
            rs_t = b.op("scalar", "activation", ln4[:, 4:5], ln4[:, 3:4], AF.Exp,
                        scale=-0.5, sem=s_act)
            b.wait("vector", s_act, rs_t)
            o_t = b.op("vector", "tensor_scalar", outb[:, 0:E],
                       motm[:, 0:E], ln4[:, 4:5], None, ALU.mult,
                       sem=s_ln)
            ln2_t[a] = o_t
            b.wait("sync", s_ln, o_t)
            ost_t[a] = b.dma("sync", s_out, dout[a * 128:(a + 1) * 128, :],
                             outb[:, 0:E])

        if "p3" in DEBUG_TAPS:
            dh1 = dbg_out("dbg_h1", [TOK, E])
            dct = dbg_out("dbg_cT", [8, TOK], BF16)
            b.gate(engs=("sync",))
            b.dma("sync", s_st, dh1[:, :], h1_dram[:, :])
            b.dma("sync", s_st, dct[:, :], cT_dram[:, :])
        if "p4" in DEBUG_TAPS:
            dmo = dbg_out("dbg_mo", [128, NEC * TOK])
            b.gate(engs=("sync",))
            b.dma("sync", s_st, dmo[:, :], mo_sb[:])
        b.wait("sync", s_out, 8 * 16)
        b.wait("sync", s_st, s_st.n)

    p4s.close()
    b.audit()
    b.emit(cut=cut)
    st.close()
    return nc, dbg


def tp3_last(k, otm_t):
    return otm_t[8 * (k // 2) + 4 * (k % 2) + 3][0]


_NC_CACHE = None


def _get_nc():
    global _NC_CACHE
    if _NC_CACHE is None:
        _NC_CACHE = build_nc()
    return _NC_CACHE


def kernel(**kw):
    x = np.ascontiguousarray(np.asarray(kw["L0_attn_x"], dtype=np.float32).reshape(8192, E))
    Wq = np.asarray(kw["L0_attn_Wq"], np.float32)
    bq = np.asarray(kw["L0_attn_bq"], np.float32)
    Wk = np.asarray(kw["L0_attn_Wk"], np.float32)
    bk = np.asarray(kw["L0_attn_bk"], np.float32)
    Wv = np.asarray(kw["L0_attn_Wv"], np.float32)
    bv = np.asarray(kw["L0_attn_bv"], np.float32)
    Wo = np.asarray(kw["L0_attn_Wo"], np.float32)
    bo = np.asarray(kw["L0_attn_bo"], np.float32)
    gw = np.asarray(kw["L0_ffn_gate_w"], np.float32)
    w1 = np.asarray(kw["L0_ffn_w1"], np.float32)
    w2 = np.asarray(kw["L0_ffn_w2"], np.float32)

    perm_head = np.concatenate([np.arange(0, 128, 2), np.arange(1, 128, 2)])
    perm = (np.arange(H)[:, None] * 128 + perm_head[None, :]).reshape(-1)

    common = {
        "wqt": np.ascontiguousarray(Wq.T[:, perm]),
        "wkt": np.ascontiguousarray(Wk.T[:, perm]),
        "wvt": np.ascontiguousarray(Wv.T),
        "wot": np.ascontiguousarray(Wo.T),
        "bq": np.ascontiguousarray(bq[perm]).reshape(1, E),
        "bk": np.ascontiguousarray(bk[perm]).reshape(1, E),
        "bv": bv.reshape(1, E).copy(),
        "bot": np.ascontiguousarray(bo.reshape(NEC, 128).T),
        "gwt": np.ascontiguousarray(gw.T.reshape(NEC, 128, NE).transpose(1, 0, 2).reshape(128, 128)),
        "w1t": np.ascontiguousarray(w1.transpose(0, 2, 1)),
        "w2t": np.ascontiguousarray(w2.transpose(0, 2, 1)),
    }

    j64 = np.arange(64, dtype=np.float64)
    theta = 10000.0 ** (-2.0 * j64 / 128.0)

    in_maps = []
    for c in range(NCORE):
        rows = np.concatenate(
            [np.arange(256 * (8 * j + c), 256 * (8 * j + c) + 256) for j in range(4)])
        ang = rows[None, :].astype(np.float64) * theta[:, None]
        cos_t = np.cos(ang)
        sin_t = np.sin(ang)
        thr = np.zeros((128, NSLOT_TOT), np.float32)
        s = 0
        for j in range(4):
            qb = 256 * (8 * j + c)
            for i in range(16 * (j + 1)):
                thr[:, s] = qb - 128 * i
                s += 1
        m = dict(common)
        m["x"] = np.ascontiguousarray(x[rows])
        m["cosx"] = np.ascontiguousarray(
            np.concatenate([cos_t, cos_t], 0).astype(np.float32))
        m["sinx"] = np.ascontiguousarray(
            np.concatenate([-sin_t, sin_t], 0).astype(np.float32))
        m["thr"] = thr
        in_maps.append(m)

    nc, _dbg = _get_nc()
    res = run_bass_kernel_spmd(nc, in_maps, core_ids=list(range(NCORE)))
    out = np.empty((8192, E), np.float32)
    for c in range(NCORE):
        oc = np.asarray(res.results[c]["out"])
        for j in range(4):
            bidx = 8 * j + c
            out[256 * bidx:256 * bidx + 256] = oc[256 * j:256 * j + 256]
    return out.reshape(1, 8192, E)


# version 03 (final)
# speedup vs baseline: 1.0062x; 1.0062x over previous
import numpy as np
import ml_dtypes
from contextlib import ExitStack

import concourse.bass as bass
import concourse.mybir as mybir
from concourse.bass_utils import run_bass_kernel_spmd

F32 = mybir.dt.float32
BF16 = mybir.dt.bfloat16
AF = mybir.ActivationFunctionType
ALU = mybir.AluOpType
AX = mybir.AxisListType
BF = ml_dtypes.bfloat16

NCORE = 8
E = 2048
TOK = 1024
H = 16
NEC = 16
NE = 8
I1 = 1408
NOC = 22
NIC = 11
SCALE = float(1.0 / np.sqrt(128.0))
SHIFT = -30.0
EPS = 1e-5
NSLOT = [16, 32, 48, 64]
SLOT0 = [0, 16, 48, 96]
NSLOT_TOT = 160

ENGS = ("sync", "gpsimd", "scalar", "vector", "tensor")
import os as _os
QK_HEADS = int(_os.environ.get("QK_HEADS", "16"))
P3_LN = _os.environ.get("P3_LN", "full")
DEBUG_TAPS = tuple(x for x in _os.environ.get("DEBUG_TAPS", "").split(",") if x)
QK_STAGE = _os.environ.get("QK_STAGE", "full")
P0_MODE = _os.environ.get("P0_MODE", "full")


class Sem:
    __slots__ = ("h", "n", "name")

    def __init__(self, h, name):
        self.h = h
        self.n = 0
        self.name = name


class Build:
    def __init__(self, nc, stack):
        self.nc = nc
        self.stack = stack
        self.q = {e: [] for e in ENGS}
        self.sems = []
        self.waits = []
        self.marks = []

    def mark(self):
        self.marks.append({e: len(self.q[e]) for e in ENGS})

    def sem(self, name):
        s = Sem(self.stack.enter_context(self.nc.semaphore(name)), name)
        self.sems.append(s)
        return s

    def wait(self, eng, sem, tgt=None):
        v = sem.n if tgt is None else tgt
        if v <= 0:
            return
        self.waits.append((sem, v))
        self.q[eng].append(lambda e, h=sem.h, v=v: e.wait_ge(h, v))

    def dma(self, eng, sem, out_ap, in_ap, **kw):
        sem.n += 16
        self.q[eng].append(
            lambda e, o=out_ap, i=in_ap, h=sem.h, kw=kw:
                e.dma_start(out=o, in_=i, **kw).then_inc(h, 16))
        return sem.n

    def op(self, eng, method, *args, sem=None, sem_bare=False, **kw):
        tgt = None
        if sem is not None:
            sem.n += 1
            tgt = sem.n

        def f(e, m=method, a=args, kw=kw, s=sem, bare=sem_bare):
            ins = getattr(e, m)(*a, **kw)
            if s is not None:
                if bare:
                    ins.then_inc(s.h)
                else:
                    ins.then_inc(s.h, 1)
        self.q[eng].append(f)
        return tgt

    def gate(self, engs=ENGS, exclude=()):
        snap = [(s, s.n) for s in self.sems if s.n > 0 and s not in exclude]
        for eng in engs:
            for s, v in snap:
                self.wait(eng, s, v)

    def audit(self):
        for s, v in self.waits:
            assert v <= s.n, f"wait target {v} > final {s.n} on sem {s.name}"

    def emit(self, cut=None):
        blk = self.stack.enter_context(self.nc.Block())
        for eng in ENGS:
            fns = self.q[eng]
            if cut is not None and cut < len(self.marks):
                fns = fns[:self.marks[cut][eng]]
            getattr(blk, eng)(lambda e, fns=fns: [f(e) for f in fns])


def build_nc(cut=None):
    nc = bass.Bass("TRN2", target_bir_lowering=False, debug=False)

    dx = nc.dram_tensor("x", [TOK, E], F32, kind="ExternalInput").ap()
    dwqt = nc.dram_tensor("wqt", [E, E], F32, kind="ExternalInput").ap()
    dwkt = nc.dram_tensor("wkt", [E, E], F32, kind="ExternalInput").ap()
    dwvt = nc.dram_tensor("wvt", [E, E], F32, kind="ExternalInput").ap()
    dwot = nc.dram_tensor("wot", [E, E], F32, kind="ExternalInput").ap()
    dbq = nc.dram_tensor("bq", [1, E], F32, kind="ExternalInput").ap()
    dbk = nc.dram_tensor("bk", [1, E], F32, kind="ExternalInput").ap()
    dbv = nc.dram_tensor("bv", [1, E], F32, kind="ExternalInput").ap()
    dbot = nc.dram_tensor("bot", [128, 16], F32, kind="ExternalInput").ap()
    dgwt = nc.dram_tensor("gwt", [128, 128], F32, kind="ExternalInput").ap()
    dw1t = nc.dram_tensor("w1t", [NE, E, 2 * I1], F32, kind="ExternalInput").ap()
    dw2t = nc.dram_tensor("w2t", [NE, I1, E], F32, kind="ExternalInput").ap()
    dcos = nc.dram_tensor("cosx", [128, TOK], F32, kind="ExternalInput").ap()
    dsin = nc.dram_tensor("sinx", [128, TOK], F32, kind="ExternalInput").ap()
    dthr = nc.dram_tensor("thr", [128, NSLOT_TOT], F32, kind="ExternalInput").ap()
    dout = nc.dram_tensor("out", [TOK, E], F32, kind="ExternalOutput").ap()

    kag_in = nc.dram_tensor("kag_in", [4 * H * 128, 256], BF16, kind="Internal").ap()
    kag_out = nc.dram_tensor("kag_out", [NCORE * 4 * H * 128, 256], BF16, kind="Internal").ap()
    vag_in = nc.dram_tensor("vag_in", [TOK, E], BF16, kind="Internal").ap()
    vag_out = nc.dram_tensor("vag_out", [NCORE * TOK, E], BF16, kind="Internal").ap()
    h1_dram = nc.dram_tensor("h1_dram", [TOK, E], F32, kind="Internal").ap()
    cT_dram = nc.dram_tensor("cT_dram", [8, TOK], BF16, kind="Internal").ap()

    ident_b = nc.inline_tensor(np.eye(128, dtype=BF), "ident_b")
    ident_f = nc.inline_tensor(np.eye(128, dtype=np.float32), "ident_f")
    Rnp = np.zeros((128, 128), dtype=np.float32)
    for r in range(64):
        Rnp[64 + r, r] = -1.0
        Rnp[r, 64 + r] = 1.0
    R_c = nc.inline_tensor(Rnp.astype(BF), "Rmat")
    ones_c = nc.inline_tensor(np.ones((1, 512), dtype=BF), "ones_row")
    onesm_c = nc.inline_tensor(np.ones((128, 128), dtype=BF), "ones_mat")
    D_c = nc.inline_tensor(
        (np.arange(128, dtype=np.float32)[:, None]
         - np.arange(256, dtype=np.float32)[None, :]), "Dtab")

    st = ExitStack()
    A = st.enter_context
    b = Build(nc, st)

    dbg = {}

    def dbg_out(name, shape, dtype=F32):
        dbg[name] = nc.dram_tensor(name, shape, dtype, kind="ExternalOutput").ap()
        return dbg[name]

    id_b = A(nc.sbuf_tensor("id_b", [128, 128], BF16, side="left"))
    id_f = A(nc.sbuf_tensor("id_f", [128, 128], F32, side="left"))
    Rm = A(nc.sbuf_tensor("Rm", [128, 128], BF16, side="left"))
    onesr = A(nc.sbuf_tensor("onesr", [1, 512], BF16, side="left"))
    onesm = A(nc.sbuf_tensor("onesm", [128, 128], BF16, side="left"))
    Dt = A(nc.sbuf_tensor("Dt", [128, 256], F32, side="left"))
    thr_sb = A(nc.sbuf_tensor("thr_sb", [128, NSLOT_TOT], F32, side="left"))
    bq_sb = A(nc.sbuf_tensor("bq_sb", [1, E], BF16, side="left"))
    bk_sb = A(nc.sbuf_tensor("bk_sb", [1, E], BF16, side="left"))
    bv_sb = A(nc.sbuf_tensor("bv_sb", [1, E], BF16, side="left"))
    bot_sb = A(nc.sbuf_tensor("bot_sb", [128, 16], F32, side="left"))
    eps_sb = A(nc.sbuf_tensor("eps_sb", [128, 1], F32, side="left"))
    shf_sb = A(nc.sbuf_tensor("shf_sb", [128, 1], F32, side="left"))
    cT = A(nc.sbuf_tensor("cT", [8, TOK], BF16, side="left"))

    pQ = ExitStack()
    Qsb = pQ.enter_context(nc.sbuf_tensor("Qsb", [128, H * TOK], BF16, side="right"))

    s_ld = b.sem("s_ld")
    s_gld = b.sem("s_gld")
    s_x = b.sem("s_x")
    s_xb = b.sem("s_xb")
    s_tp = b.sem("s_tp")
    s_xT = b.sem("s_xT")
    s_w = b.sem("s_w")
    s_mm = b.sem("s_mm")
    s_ybf = b.sem("s_ybf")
    s_sh = b.sem("s_sh")
    s_t2 = b.sem("s_t2")
    s_rot = b.sem("s_rot")
    s_ve = b.sem("s_ve")
    s_st = b.sem("s_st")
    s_ccK = b.sem("s_ccK")
    s_ccV = b.sem("s_ccV")
    s_kd = b.sem("s_kd")
    s_vd = b.sem("s_vd")
    s_S = b.sem("s_S")
    s_E = b.sem("s_E")
    s_Mv = b.sem("s_Mv")
    s_Mg = b.sem("s_Mg")
    s_P = b.sem("s_P")
    s_F = b.sem("s_F")
    s_ln = b.sem("s_ln")
    s_act = b.sem("s_act")
    s_g = b.sem("s_g")
    s_mo = b.sem("s_mo")
    s_ap = b.sem("s_ap")
    s_out = b.sem("s_out")
    dsem = [b.sem(f"d{i}") for i in range(8)]
    swsem = [b.sem("sw0"), b.sem("sw1")]

    p01 = ExitStack()
    xT = p01.enter_context(nc.sbuf_tensor("xT", [128, NEC * TOK], BF16, side="right"))
    Ksb = p01.enter_context(nc.sbuf_tensor("Ksb", [128, H * TOK], BF16, side="right"))
    cos_sb = p01.enter_context(nc.sbuf_tensor("cos_sb", [128, TOK], F32, side="right"))
    sin_sb = p01.enter_context(nc.sbuf_tensor("sin_sb", [128, TOK], F32, side="right"))

    b.dma("sync", s_ld, id_b[:], ident_b.ap())
    b.dma("sync", s_ld, id_f[:], ident_f.ap())
    b.dma("sync", s_ld, Rm[:], R_c.ap())
    b.dma("sync", s_ld, onesr[:], ones_c.ap())
    b.dma("sync", s_ld, onesm[:], onesm_c.ap())
    b.dma("sync", s_ld, Dt[:], D_c.ap())
    b.dma("sync", s_ld, thr_sb[:], dthr[:, :])
    b.dma("sync", s_ld, bot_sb[:], dbot[:, :])
    b.dma("sync", s_ld, cos_sb[:], dcos[:, :])
    b.dma("sync", s_ld, sin_sb[:], dsin[:, :])
    LD_CONST = s_ld.n
    b.dma("gpsimd", s_gld, bq_sb[:], dbq[:, :])
    b.dma("gpsimd", s_gld, bk_sb[:], dbk[:, :])
    b.dma("gpsimd", s_gld, bv_sb[:], dbv[:, :])
    GLD_BIAS = s_gld.n
    b.op("vector", "memset", eps_sb[:, 0:1], EPS, sem=s_ve)
    b.op("vector", "memset", shf_sb[:, 0:1], SHIFT, sem=s_ve)
    MEMSET_VE = s_ve.n

    with nc.sbuf_tensor("xf0", [128, 2 * E], F32, side="right") as xf0, \
         nc.sbuf_tensor("xb0", [128, 2 * E], BF16, side="right") as xb0, \
         nc.psum_tensor("pt0", [128, 1024], BF16) as pt0, \
         nc.psum_tensor("pt1", [128, 1024], BF16) as pt1:
        pts = (pt0, pt1)
        cast_t = {}
        tp_t = {}
        ev_t = {}
        xld_t = {}
        for a in (range(0) if P0_MODE == "none" else range(8)):
            par = a % 2
            if a >= 2:
                b.wait("sync", s_xb, cast_t[a - 2])
            xld_t[a] = b.dma("sync", dsem[par], xf0[:, par * E:(par + 1) * E],
                             dx[a * 128:(a + 1) * 128, :])
            b.wait("vector", dsem[par], xld_t[a])
            if a >= 2 and P0_MODE != "loadonly":
                b.wait("vector", s_tp, tp_t[(a - 2) * NEC + NEC - 1])
            cast_t[a] = b.op("vector", "tensor_copy",
                             xb0[:, par * E:(par + 1) * E],
                             xf0[:, par * E:(par + 1) * E], sem=s_xb)
            if P0_MODE == "loadonly":
                continue
            b.wait("tensor", s_xb, cast_t[a])
            if a == 0:
                b.wait("tensor", s_ld, LD_CONST)
            for ec in range(NEC):
                k = a * NEC + ec
                pt = pts[k % 2]
                col = ((k // 2) % 8) * 128
                if k >= 2:
                    b.wait("tensor", s_xT, ev_t[k - 2])
                tp_t[k] = b.op("tensor", "transpose", pt[:, col:col + 128],
                               xb0[:, par * E + ec * 128: par * E + (ec + 1) * 128],
                               id_b[:], sem=s_tp)
                b.wait("scalar", s_tp, tp_t[k])
                ev_t[k] = b.op("scalar", "copy",
                               xT[:, ec * TOK + a * 128: ec * TOK + (a + 1) * 128],
                               pt[:, col:col + 128], sem=s_xT)

    if _os.environ.get("P0_GATE", "on") == "on":
        b.gate()
    b.mark()

    def qk_proj(which, dw, bias_sb, dst):
        with nc.sbuf_tensor(f"wcol_{which}", [128, 2 * NEC * 128], BF16, side="right") as wcol, \
             nc.sbuf_tensor(f"y32_{which}", [128, 2 * 512], F32, side="right") as y32, \
             nc.sbuf_tensor(f"ysh_{which}", [128, 2 * 512], F32, side="right") as ysh, \
             nc.sbuf_tensor(f"t1_{which}", [128, 2 * 512], F32, side="right") as t1sb, \
             nc.sbuf_tensor(f"t2_{which}", [128, 2 * 512], F32, side="right") as t2sb, \
             nc.psum_tensor(f"pqk0_{which}", [128, 512], F32) as pqk0, \
             nc.psum_tensor(f"pqk1_{which}", [128, 512], F32) as pqk1:
            pqks = (pqk0, pqk1)
            wtgt = {}
            mm_t = {}
            ev_t = {}
            sh_t = {}
            t2_t = {}
            rot_t = {}
            for h in range(QK_HEADS):
                wpar = h % 2
                if h >= 2:
                    b.wait("gpsimd", s_mm, mm_t[2 * (h - 2) + 1])
                for sp in range(4):
                    wtgt[h] = b.dma(
                        "gpsimd", swsem[wpar],
                        wcol[:, (wpar * NEC + sp * 4) * 128:(wpar * NEC + (sp + 1) * 4) * 128].rearrange(
                            "p (ec f) -> p ec f", ec=4),
                        dw[sp * 512:(sp + 1) * 512, h * 128:(h + 1) * 128].rearrange(
                            "(ec p) f -> p ec f", p=128))
                for tt in range(2):
                    k = 2 * h + tt
                    par = k % 2
                    b.wait("tensor", swsem[wpar], wtgt[h])
                    if k >= 2:
                        b.wait("tensor", s_ybf, ev_t[k - 2])
                    for ec in range(NEC):
                        b.op("tensor", "matmul",
                             pqks[par][:, 0:512],
                             wcol[:, (wpar * NEC + ec) * 128:(wpar * NEC + ec + 1) * 128],
                             xT[:, ec * TOK + tt * 512: ec * TOK + tt * 512 + 512],
                             start=(ec == 0), stop=False)
                    mm_t[k] = b.op(
                        "tensor", "matmul", pqks[par][:, 0:512],
                        bias_sb[0:1, h * 128:(h + 1) * 128], onesr[0:1, :],
                        start=False, stop=True, sem=s_mm)
                    b.wait("scalar", s_mm, mm_t[k])
                    if k >= 2:
                        b.wait("scalar", s_t2, t2_t[k - 2])
                        b.wait("scalar", dsem[par], sh_t[k - 2])
                    ev_t[k] = b.op("scalar", "copy",
                                   y32[:, par * 512:(par + 1) * 512],
                                   pqks[par][:, 0:512], sem=s_ybf)
                    b.wait("sync", s_ybf, ev_t[k])
                    if k >= 2:
                        b.wait("sync", s_t2, t2_t[k - 2])
                    b.dma("sync", dsem[par],
                          ysh[64:128, par * 512:(par + 1) * 512],
                          y32[0:64, par * 512:(par + 1) * 512])
                    sh_t[k] = b.dma("sync", dsem[par],
                                    ysh[0:64, par * 512:(par + 1) * 512],
                                    y32[64:128, par * 512:(par + 1) * 512])
                    b.wait("vector", s_ybf, ev_t[k])
                    if k >= 2:
                        b.wait("vector", s_rot, rot_t[k - 2])
                    b.op("vector", "tensor_tensor", t1sb[:, par * 512:(par + 1) * 512],
                         y32[:, par * 512:(par + 1) * 512],
                         cos_sb[:, tt * 512:(tt + 1) * 512], ALU.mult)
                    b.wait("vector", dsem[par], sh_t[k])
                    t2_t[k] = b.op(
                        "vector", "tensor_tensor", t2sb[:, par * 512:(par + 1) * 512],
                        ysh[:, par * 512:(par + 1) * 512],
                        sin_sb[:, tt * 512:(tt + 1) * 512], ALU.mult, sem=s_t2)
                    b.wait("gpsimd", s_t2, t2_t[k])
                    rot_t[k] = b.op("gpsimd", "tensor_tensor",
                                    dst[:, h * TOK + tt * 512: h * TOK + tt * 512 + 512],
                                    t1sb[:, par * 512:(par + 1) * 512],
                                    t2sb[:, par * 512:(par + 1) * 512], ALU.add, sem=s_rot)

    b.wait("gpsimd", s_gld, GLD_BIAS)
    b.wait("tensor", s_ld, LD_CONST)
    b.wait("vector", s_ld, LD_CONST)
    qk_proj("k", dwkt, bk_sb, Ksb)
    b.mark()
    k_rot_final = s_rot.n
    b.wait("sync", s_rot, k_rot_final)
    for bb in range(4):
        for h in range(H):
            b.dma("sync", s_st, kag_in[(bb * H + h) * 128:(bb * H + h + 1) * 128, :],
                  Ksb[:, h * TOK + bb * 256: h * TOK + (bb + 1) * 256])
    kst_final = s_st.n
    b.mark()

    b.gate(engs=("tensor", "scalar", "vector", "gpsimd"))
    with nc.sbuf_tensor("wvcol", [128, 2 * NEC * 512], BF16, side="right") as wvcol, \
         nc.sbuf_tensor("vsb", [128, 2 * 512], BF16, side="right") as vsb, \
         nc.psum_tensor("pv0", [128, 512], F32) as pv0, \
         nc.psum_tensor("pv1", [128, 512], F32) as pv1:
        pvs = (pv0, pv1)
        wv_t = {}
        vmm_t = {}
        vev_t = {}
        vst_t = {}
        for ft in range(4):
            wpar = ft % 2
            if ft >= 2:
                b.wait("gpsimd", s_mm, vmm_t[(ft - 2) * 8 + 7])
            for sp in range(4):
                wv_t[ft] = b.dma(
                    "gpsimd", swsem[wpar],
                    wvcol[:, (wpar * NEC + sp * 4) * 512:(wpar * NEC + (sp + 1) * 4) * 512].rearrange(
                        "p (ec f) -> p ec f", ec=4),
                    dwvt[sp * 512:(sp + 1) * 512, ft * 512:(ft + 1) * 512].rearrange(
                        "(ec p) f -> p ec f", p=128))
            for a in range(8):
                k = ft * 8 + a
                par = k % 2
                b.wait("tensor", swsem[wpar], wv_t[ft])
                if k >= 2:
                    b.wait("tensor", s_ve, vev_t[k - 2])
                for ec in range(NEC):
                    b.op("tensor", "matmul", pvs[par][:, 0:512],
                         xT[:, ec * TOK + a * 128: ec * TOK + (a + 1) * 128],
                         wvcol[:, (wpar * NEC + ec) * 512:(wpar * NEC + ec + 1) * 512],
                         start=(ec == 0), stop=False)
                vmm_t[k] = b.op(
                    "tensor", "matmul", pvs[par][:, 0:512],
                    onesr[0:1, 0:128], bv_sb[0:1, ft * 512:(ft + 1) * 512],
                    start=False, stop=True, sem=s_mm)
                b.wait("scalar", s_mm, vmm_t[k])
                if k >= 2:
                    b.wait("scalar", dsem[6 + par], vst_t[k - 2])
                vev_t[k] = b.op(
                    "scalar", "copy", vsb[:, par * 512:(par + 1) * 512],
                    pvs[par][:, 0:512], sem=s_ve)
                b.wait("sync", s_ve, vev_t[k])
                vst_t[k] = b.dma("sync", dsem[6 + par],
                                 vag_in[a * 128:(a + 1) * 128, ft * 512:(ft + 1) * 512],
                                 vsb[:, par * 512:(par + 1) * 512])
        vst6, vst7 = dsem[6].n, dsem[7].n

    b.mark()
    b.wait("gpsimd", s_st, kst_final)
    b.wait("gpsimd", dsem[6], vst6)
    b.wait("gpsimd", dsem[7], vst7)
    b.op("gpsimd", "collective_compute", "AllGather", ALU.bypass,
         [list(range(NCORE))], [kag_in[:, :]], [kag_out[:, :]],
         sem=s_ccK, sem_bare=True)
    b.op("gpsimd", "collective_compute", "AllGather", ALU.bypass,
         [list(range(NCORE))], [vag_in[:, :]], [vag_out[:, :]],
         sem=s_ccV, sem_bare=True)
    b.mark()

    b.gate(engs=("tensor", "scalar", "vector", "gpsimd"), exclude=(s_ccK, s_ccV))
    qk_proj("q", dwqt, bq_sb, Qsb)

    if "p1" in DEBUG_TAPS:
        dq = dbg_out("dbg_Q", [128, H * TOK], BF16)
        dk = dbg_out("dbg_K", [4 * H * 128, 256], BF16)
        dv = dbg_out("dbg_V", [TOK, E], BF16)
        b.gate(engs=("sync",), exclude=(s_ccK, s_ccV))
        b.dma("sync", s_st, dq[:, :], Qsb[:])
        b.dma("sync", s_st, dk[:, :], kag_in[:, :])
        b.dma("sync", s_st, dv[:, :], vag_in[:, :])

    p01.close()
    p23 = ExitStack()
    attnT = p23.enter_context(
        nc.sbuf_tensor("attnT", [128, NEC * TOK], BF16, side="left"))
    b.gate()
    b.wait("sync", s_ccK, 1)
    b.wait("sync", s_ccV, 1)
    b.wait("tensor", s_ccK, 1)
    b.wait("tensor", s_ccV, 1)
    b.mark()

    with nc.sbuf_tensor("maskb", [128, 64 * 256], BF16, side="right") as maskb, \
         nc.sbuf_tensor("kbuf", [128, 4 * 1024], BF16, side="right") as kbuf, \
         nc.sbuf_tensor("vbuf", [128, 4 * 512], BF16, side="right") as vbuf, \
         nc.sbuf_tensor("ptb", [128, 2 * 1024], BF16, side="right") as ptb, \
         nc.sbuf_tensor("recip", [128, 1024], F32, side="right") as recip, \
         nc.sbuf_tensor("den_sb", [128, 1024], F32, side="right") as den_sb, \
         nc.sbuf_tensor("acc_sb", [128, 1024], F32, side="right") as acc_sb, \
         nc.psum_tensor("sregA", [128, 1024], F32) as sregA, \
         nc.psum_tensor("sregB", [128, 1024], F32) as sregB, \
         nc.psum_tensor("pvacc", [128, 1024], F32) as pvacc, \
         nc.psum_tensor("pden", [128, 1024], F32) as pden:
        sregs = (sregA, sregB)

        Sg, Eg, Pg, Vg, Kg, Mgt, Mge = [], [], [], [], [], [], []
        f_t = {}
        seq = [(j, g) for j in range(4) for g in range(4)]
        glob = 0
        globp = 0

        for si, (j, g) in enumerate(seq):
            n = NSLOT[j]
            for i in range(n):
                G = glob + i
                par = G % 2
                half = i % 2
                gb = i // 2
                r, jl = gb % 8, gb // 8
                ksl = (globp + gb) % 4
                vsl = G % 4
                eng = "vector" if G % 2 == 0 else "gpsimd"
                sm = s_Mv if G % 2 == 0 else s_Mg

                if i % 2 == 0:
                    P = globp + gb
                    if P >= 4:
                        b.wait("sync", s_S, Sg[2 * (P - 4) + 1])
                    Kg.append(b.dma(
                        "sync", dsem[P % 4],
                        kbuf[:, ksl * 1024:(ksl + 1) * 1024].rearrange(
                            "d (h k) -> d h k", h=4),
                        kag_out[(r * 4 + jl) * H * 128 + 4 * g * 128:
                                (r * 4 + jl) * H * 128 + 4 * (g + 1) * 128, :].rearrange(
                            "(h d) k -> d h k", d=128)))
                if G >= 4:
                    b.wait("sync", s_P, Pg[G - 4])
                Vg.append(b.dma(
                    "sync", dsem[4 + G % 4], vbuf[:, vsl * 512:(vsl + 1) * 512],
                    vag_out[r * TOK + jl * 256 + half * 128:
                            r * TOK + jl * 256 + half * 128 + 128,
                            4 * g * 128:4 * (g + 1) * 128]))

                b.wait("tensor", dsem[(globp + gb) % 4], Kg[globp + gb])
                if G >= 2:
                    b.wait("tensor", s_E, Eg[G - 2])
                for u in range(4):
                    kw = {}
                    if u == 3:
                        kw["sem"] = s_S
                    b.op("tensor", "matmul",
                         sregs[par][:, u * 256:(u + 1) * 256],
                         kbuf[:, ksl * 1024 + u * 256 + half * 128:
                              ksl * 1024 + u * 256 + half * 128 + 128],
                         Qsb[:, (4 * g + u) * TOK + 256 * j:
                             (4 * g + u) * TOK + 256 * (j + 1)],
                         start=True, stop=True, **kw)
                Sg.append(s_S.n)

                b.wait("scalar", s_S, Sg[G])
                if G >= 2:
                    b.wait("scalar", s_P, Pg[G - 2])
                if G == 0:
                    b.wait("scalar", s_ve, MEMSET_VE)
                Eg.append(b.op(
                    "scalar", "activation", ptb[:, par * 1024:(par + 1) * 1024],
                    sregs[par][:, 0:1024], AF.Exp,
                    bias=shf_sb[:, 0:1], scale=SCALE, sem=s_E))

                if g == 0:
                    b.wait(eng, s_ld, LD_CONST)
                    mg_t = b.op(eng, "tensor_scalar", maskb[:, i * 256:(i + 1) * 256],
                                Dt[:], thr_sb[:, SLOT0[j] + i:SLOT0[j] + i + 1], None,
                                ALU.is_le, sem=sm)
                    b.wait(eng, sm, mg_t)
                b.wait(eng, s_E, Eg[G])
                for u in range(4):
                    kw = {}
                    if u == 3:
                        kw["sem"] = sm
                    b.op(eng, "tensor_tensor",
                         ptb[:, par * 1024 + u * 256: par * 1024 + (u + 1) * 256],
                         ptb[:, par * 1024 + u * 256: par * 1024 + (u + 1) * 256],
                         maskb[:, i * 256:(i + 1) * 256], ALU.mult, **kw)
                if G % 2 == 0:
                    Mgt.append(sm.n)
                    Mge.append(None)
                else:
                    Mgt.append(sm.n)
                    Mge.append(None)

                b.wait("tensor", sm, Mgt[G])
                b.wait("tensor", dsem[4 + G % 4], Vg[G])
                if i == 0 and si > 0:
                    b.wait("tensor", s_F, f_t[seq[si - 1]])
                for u in range(4):
                    b.op("tensor", "matmul", pvacc[:, u * 256:(u + 1) * 256],
                         vbuf[:, vsl * 512 + u * 128: vsl * 512 + (u + 1) * 128],
                         ptb[:, par * 1024 + u * 256: par * 1024 + (u + 1) * 256],
                         start=(i == 0 and u in (0, 2)), stop=(i == n - 1))
                b.op("tensor", "matmul", pden[:, 0:512], onesm[:],
                     ptb[:, par * 1024: par * 1024 + 512],
                     start=(i == 0), stop=(i == n - 1))
                Pg.append(b.op(
                    "tensor", "matmul", pden[:, 512:1024], onesm[:],
                    ptb[:, par * 1024 + 512: par * 1024 + 1024],
                    start=(i == 0), stop=(i == n - 1), sem=s_P))

            b.wait("scalar", s_P, Pg[glob + n - 1])
            if si > 0:
                b.wait("scalar", s_F, f_t[seq[si - 1]])
            b.op("scalar", "copy", den_sb[:], pden[:])
            ev_fin = b.op("scalar", "copy", acc_sb[:], pvacc[:], sem=s_ybf)
            b.wait("vector", s_ybf, ev_fin)
            b.op("vector", "reciprocal", recip[:], den_sb[:])
            for u in range(4):
                kw = {}
                if u == 3:
                    kw["sem"] = s_F
                b.op("vector", "tensor_tensor",
                     attnT[:, (4 * g + u) * TOK + 256 * j:
                           (4 * g + u) * TOK + 256 * (j + 1)],
                     acc_sb[:, u * 256:(u + 1) * 256],
                     recip[:, u * 256:(u + 1) * 256], ALU.mult, **kw)
            f_t[(j, g)] = s_F.n
            glob += n
            globp += n // 2

    if "p2" in DEBUG_TAPS:
        d = dbg_out("dbg_attnT", [128, NEC * TOK], BF16)
        b.gate(engs=("sync",))
        b.dma("sync", s_st, d[:, :], attnT[:])

    pQ.close()
    b.gate()
    b.mark()

    h1T = A(nc.sbuf_tensor("h1T", [128, NEC * TOK], BF16, side="right"))

    p3a = ExitStack()
    E3 = p3a.enter_context
    po3a = E3(nc.psum_tensor("po3a", [128, 512], F32))
    po3b = E3(nc.psum_tensor("po3b", [128, 512], F32))
    po3s = (po3a, po3b)
    pt3a = E3(nc.psum_tensor("pt3a", [128, 512], F32))
    pt3b = E3(nc.psum_tensor("pt3b", [128, 512], F32))
    pt3s = (pt3a, pt3b)
    ptb3 = E3(nc.psum_tensor("ptb3", [128, 1024], BF16))
    ptb3b = E3(nc.psum_tensor("ptb3b", [128, 1024], BF16))
    otm = E3(nc.sbuf_tensor("otm", [128, 8 * E], F32, side="right"))
    p3b = ExitStack()
    wocol = p3b.enter_context(
        nc.sbuf_tensor("wocol", [128, 2 * NEC * 128], BF16, side="right"))
    osb = p3b.enter_context(
        nc.sbuf_tensor("osb", [128, 2 * 512], F32, side="right"))
    if 1:
        ptb3s = (ptb3, ptb3b)

        wo_t = {}
        omm_t = {}
        oev_t = {}
        obias_t = {}
        otm_t = {}
        for fc in range(NEC):
            wpar = fc % 2
            if fc >= 2:
                b.wait("gpsimd", s_mm, omm_t[2 * (fc - 2) + 1])
            for sp in range(4):
                wo_t[fc] = b.dma(
                    "gpsimd", swsem[wpar],
                    wocol[:, (wpar * NEC + sp * 4) * 128:(wpar * NEC + (sp + 1) * 4) * 128].rearrange(
                        "p (ec f) -> p ec f", ec=4),
                    dwot[sp * 512:(sp + 1) * 512, fc * 128:(fc + 1) * 128].rearrange(
                        "(ec p) f -> p ec f", p=128))
            for tt in range(2):
                k = 2 * fc + tt
                par = k % 2
                b.wait("tensor", swsem[wpar], wo_t[fc])
                if k >= 2:
                    b.wait("tensor", s_ybf, oev_t[k - 2])
                for ec in range(NEC):
                    kw = dict(start=(ec == 0), stop=(ec == NEC - 1))
                    if ec == NEC - 1:
                        kw["sem"] = s_mm
                    b.op("tensor", "matmul", po3s[par][:, 0:512],
                         wocol[:, (wpar * NEC + ec) * 128:(wpar * NEC + ec + 1) * 128],
                         attnT[:, ec * TOK + tt * 512: ec * TOK + tt * 512 + 512],
                         **kw)
                omm_t[k] = s_mm.n
                b.wait("scalar", s_mm, omm_t[k])
                if k >= 2:
                    b.wait("scalar", s_ve, obias_t[k - 2])
                    b.wait("scalar", s_tp, tp3_last(k - 2, otm_t))
                oac_t = b.op(
                    "scalar", "copy", osb[:, par * 512:(par + 1) * 512],
                    po3s[par][:, 0:512], sem=s_ybf)
                oev_t[k] = oac_t
                b.wait("vector", s_ybf, oac_t)
                if k >= 2:
                    b.wait("vector", s_tp, tp3_last(k - 2, otm_t))
                obias_t[k] = b.op(
                    "vector", "tensor_scalar", osb[:, par * 512:(par + 1) * 512],
                    osb[:, par * 512:(par + 1) * 512], bot_sb[:, fc:fc + 1], None,
                    ALU.add, sem=s_ve)
                b.wait("tensor", s_ve, obias_t[k])
                for q in range(4):
                    a = tt * 4 + q
                    kk = 8 * fc + 4 * tt + q
                    col3 = ((kk // 2) % 4) * 128
                    if kk >= 2:
                        b.wait("tensor", s_xT, otm_t[kk - 2][1])
                    t = b.op("tensor", "transpose", pt3s[kk % 2][:, col3:col3 + 128],
                             osb[:, par * 512 + q * 128: par * 512 + (q + 1) * 128],
                             id_f[:], sem=s_tp)
                    b.wait("scalar", s_tp, t)
                    otm_t[kk] = (t, b.op(
                        "scalar", "copy",
                        otm[:, a * E + fc * 128: a * E + (fc + 1) * 128],
                        pt3s[kk % 2][:, col3:col3 + 128], sem=s_xT))

        if "p3o" in DEBUG_TAPS or "p3q" in DEBUG_TAPS:
            do3 = dbg_out("dbg_otm", [128, 8 * E])
            b.gate(engs=("sync",))
            b.dma("sync", s_st, do3[:, :], otm[:])
            if "p3o" in DEBUG_TAPS:
                b.wait("vector", s_st, s_st.n)
        p3b.close()
        p23.close()
        xf3 = E3(nc.sbuf_tensor("xf3", [128, 2 * E], F32, side="right"))
        h1f = E3(nc.sbuf_tensor("h1f", [128, 2 * E], F32, side="right"))
        h1b = E3(nc.sbuf_tensor("h1b", [128, 2 * E], BF16, side="right"))
        sq3 = E3(nc.sbuf_tensor("sq3", [128, E], F32, side="right"))
        ln3 = E3(nc.sbuf_tensor("ln3", [128, 8], F32, side="right"))
        glog = E3(nc.sbuf_tensor("glog", [8, TOK], F32, side="right"))
        gtm = E3(nc.sbuf_tensor("gtm", [128, 64], F32, side="right"))
        top2 = E3(nc.sbuf_tensor("top2", [128, 52], F32, side="right"))
        ctmb = E3(nc.sbuf_tensor("ctmb", [128, 8], BF16, side="right"))
        gw_sb = E3(nc.sbuf_tensor("gw_sb", [128, 128], BF16, side="right"))
        ln1_t = {}
        h1st_t = {}
        h1T_t = {}
        for a in range(8):
            par = a % 2
            if a < 2:
                b.wait("sync", s_xT, otm_t[8 * NEC - 1][1])
            if a >= 2:
                b.wait("sync", s_ln, ln1_t[a - 2])
            xld = b.dma("sync", dsem[par], xf3[:, par * E:(par + 1) * E],
                        dx[a * 128:(a + 1) * 128, :])
            b.wait("vector", dsem[par], xld)
            b.wait("vector", s_xT,
                   max(otm_t[8 * fc_ + 4 * (a // 4) + (a % 4)][1] for fc_ in range(NEC)))
            st_t = b.op("vector", "scalar_tensor_tensor", h1f[:, par * E:(par + 1) * E],
                        xf3[:, par * E:(par + 1) * E], 1.0, otm[:, a * E:(a + 1) * E],
                        ALU.mult, ALU.add, sem=s_ln)
            b.wait("vector", s_ln, st_t)
            rd_t = b.op("vector", "reduce_sum", ln3[:, 0:1], h1f[:, par * E:(par + 1) * E],
                        AX.X, sem=s_ln)
            if P3_LN == "raw":
                h1w_t = st_t
                cast_t = b.op("vector", "tensor_copy", h1b[:, par * E:(par + 1) * E],
                              h1f[:, par * E:(par + 1) * E], sem=s_ln)
                ln1_t[a] = cast_t
                b.wait("sync", s_ln, h1w_t)
                h1st_t[a] = b.dma("sync", dsem[6 + par], h1_dram[a * 128:(a + 1) * 128, :],
                                  h1f[:, par * E:(par + 1) * E])
                b.wait("tensor", s_ln, cast_t)
                for ec in range(NEC):
                    kk = a * NEC + ec
                    ptq = ptb3s[kk % 2]
                    colq = ((kk // 2) % 8) * 128
                    if kk >= 2:
                        b.wait("tensor", s_xT, h1T_t[kk - 2][1])
                    t = b.op("tensor", "transpose", ptq[:, colq:colq + 128],
                             h1b[:, par * E + ec * 128: par * E + (ec + 1) * 128],
                             id_b[:], sem=s_tp)
                    b.wait("scalar", s_tp, t)
                    h1T_t[kk] = (t, b.op(
                        "scalar", "copy",
                        h1T[:, ec * TOK + a * 128: ec * TOK + (a + 1) * 128],
                        ptq[:, colq:colq + 128], sem=s_xT))
                continue
            b.wait("vector", s_ln, rd_t)
            mn_t = b.op("vector", "tensor_scalar", ln3[:, 1:2], ln3[:, 0:1], 1.0 / E,
                        None, ALU.mult, sem=s_ln)
            b.wait("vector", s_ln, mn_t)
            b.op("vector", "tensor_scalar", otm[:, a * E:(a + 1) * E],
                 h1f[:, par * E:(par + 1) * E], ln3[:, 1:2], None, ALU.subtract)
            b.op("vector", "scalar_tensor_tensor", sq3[:],
                 otm[:, a * E:(a + 1) * E], 1.0, otm[:, a * E:(a + 1) * E],
                 ALU.mult, ALU.mult)
            sq_t = b.op("vector", "reduce_sum", ln3[:, 2:3], sq3[:], AX.X, sem=s_ln)
            b.wait("scalar", s_ln, sq_t)
            if a == 0:
                b.wait("scalar", s_ve, MEMSET_VE)
            lnv_t = b.op("scalar", "activation", ln3[:, 3:4], ln3[:, 2:3], AF.Ln,
                         bias=eps_sb[:, 0:1], scale=1.0 / E, sem=s_act)
            b.wait("scalar", s_act, lnv_t)
            rs_t = b.op("scalar", "activation", ln3[:, 4:5], ln3[:, 3:4], AF.Exp,
                        scale=-0.5, sem=s_act)
            b.wait("vector", s_act, rs_t)
            if a >= 2:
                b.wait("vector", dsem[6 + par], h1st_t[a - 2])
                b.wait("vector", s_tp, h1T_t[(a - 2) * NEC + NEC - 1][0])
            h1w_t = b.op("vector", "tensor_scalar", h1f[:, par * E:(par + 1) * E],
                         otm[:, a * E:(a + 1) * E], ln3[:, 4:5], None, ALU.mult,
                         sem=s_ln)
            cast_t = b.op("vector", "tensor_copy", h1b[:, par * E:(par + 1) * E],
                          h1f[:, par * E:(par + 1) * E], sem=s_ln)
            ln1_t[a] = cast_t
            b.wait("sync", s_ln, h1w_t)
            h1st_t[a] = b.dma("sync", dsem[6 + par], h1_dram[a * 128:(a + 1) * 128, :],
                              h1f[:, par * E:(par + 1) * E])
            b.wait("tensor", s_ln, cast_t)
            for ec in range(NEC):
                kk = a * NEC + ec
                ptq = ptb3s[kk % 2]
                colq = ((kk // 2) % 8) * 128
                if kk >= 2:
                    b.wait("tensor", s_xT, h1T_t[kk - 2][1])
                t = b.op("tensor", "transpose", ptq[:, colq:colq + 128],
                         h1b[:, par * E + ec * 128: par * E + (ec + 1) * 128],
                         id_b[:], sem=s_tp)
                b.wait("scalar", s_tp, t)
                h1T_t[kk] = (t, b.op(
                    "scalar", "copy",
                    h1T[:, ec * TOK + a * 128: ec * TOK + (a + 1) * 128],
                    ptq[:, colq:colq + 128], sem=s_xT))

        b.dma("gpsimd", s_gld, gw_sb[:], dgwt[:, :])
        b.wait("tensor", s_gld, s_gld.n)
        b.wait("tensor", s_xT, h1T_t[8 * NEC - 1][1])
        b.wait("tensor", s_ybf, oev_t[2 * NEC - 1])
        for tt in range(2):
            for ec in range(NEC):
                kw = dict(start=(ec == 0), stop=(ec == NEC - 1))
                if ec == NEC - 1:
                    kw["sem"] = s_mm
                b.op("tensor", "matmul", po3s[tt][0:8, 0:512],
                     gw_sb[:, ec * 8:(ec + 1) * 8],
                     h1T[:, ec * TOK + tt * 512: ec * TOK + tt * 512 + 512], **kw)
        gmm_t = s_mm.n
        b.wait("scalar", s_mm, gmm_t)
        b.op("scalar", "copy", glog[0:8, 0:512], po3s[0][0:8, 0:512])
        ge_t = b.op("scalar", "copy", glog[0:8, 512:1024], po3s[1][0:8, 0:512], sem=s_ve)
        b.wait("tensor", s_ve, ge_t)
        b.wait("tensor", s_xT, otm_t[8 * NEC - 1][1])
        gev = {}
        for a in range(8):
            if a >= 2:
                b.wait("tensor", s_ve, gev[a - 2])
            t = b.op("tensor", "transpose", pt3s[a % 2][:, 0:8],
                     glog[0:8, a * 128:(a + 1) * 128], id_f[0:8, 0:8], sem=s_tp)
            b.wait("scalar", s_tp, t)
            gev[a] = b.op("scalar", "copy", gtm[:, a * 8:(a + 1) * 8],
                          pt3s[a % 2][:, 0:8], sem=s_ve)
        ct_ev = {}
        for a in range(8):
            ga = gtm[:, a * 8:(a + 1) * 8]
            ism = top2[:, 0:8]
            msk2 = top2[:, 8:16]
            iss = top2[:, 16:24]
            sel = top2[:, 24:32]
            ex = top2[:, 32:40]
            cu = top2[:, 40:48]
            m1n = top2[:, 48:49]
            m2n = top2[:, 49:50]
            den = top2[:, 50:51]
            rec = top2[:, 51:52]
            b.wait("vector", s_ve, gev[a])
            def _edge(t):
                b.wait("vector", s_g, t)
            r1 = b.op("vector", "reduce_max", m1n, ga, AX.X, negate=True, sem=s_g)
            _edge(r1)
            _edge(b.op("vector", "tensor_scalar", ism, ga, m1n, 0.0, ALU.add,
                       ALU.is_ge, sem=s_g))
            _edge(b.op("vector", "scalar_tensor_tensor", msk2, ism, -1e30, ga,
                       ALU.mult, ALU.add, sem=s_g))
            _edge(b.op("vector", "reduce_max", m2n, msk2, AX.X, negate=True, sem=s_g))
            _edge(b.op("vector", "tensor_scalar", iss, msk2, m2n, 0.0, ALU.add,
                       ALU.is_ge, sem=s_g))
            sel_t = b.op("vector", "tensor_tensor", sel, ism, iss, ALU.add, sem=s_g)
            _edge(sel_t)
            b.wait("scalar", s_g, r1)
            ex_t = b.op("scalar", "activation", ex, ga, AF.Exp, bias=m1n, sem=s_act)
            b.wait("vector", s_act, ex_t)
            _edge(b.op("vector", "tensor_tensor", cu, sel, ex, ALU.mult, sem=s_g))
            _edge(b.op("vector", "reduce_sum", den, cu, AX.X, sem=s_g))
            _edge(b.op("vector", "reciprocal", rec, den, sem=s_g))
            cb_t = b.op("vector", "tensor_scalar", ctmb[:, 0:8], cu, rec, None,
                        ALU.mult, sem=s_g)
            b.wait("tensor", s_g, cb_t)
            ptc = ptb3s[a % 2]
            if a >= 2:
                b.wait("tensor", s_ve, ct_ev[a - 2])
            ct = b.op("tensor", "transpose", ptc[0:8, 0:128],
                      ctmb[:, 0:8], id_b[:], sem=s_tp)
            b.wait("scalar", s_tp, ct)
            ct_ev[a] = b.op("scalar", "copy", cT[0:8, a * 128:(a + 1) * 128],
                            ptc[0:8, 0:128], sem=s_ve)

        b.wait("sync", s_ve, ct_ev[7])
        b.dma("sync", s_st, cT_dram[:, :], cT[0:8, :])

    p3a.close()
    b.gate()
    b.mark()

    p4s = ExitStack()
    E4 = p4s.enter_context
    w1col = E4(nc.sbuf_tensor("w1col", [128, 2 * NEC * 128], BF16, side="right"))
    w2col = E4(nc.sbuf_tensor("w2col", [128, 2 * NIC * 128], BF16, side="right"))
    silu_sb = E4(nc.sbuf_tensor("silu_sb", [128, 4 * 512], BF16, side="right"))
    tmpb = E4(nc.sbuf_tensor("tmpb", [128, 2 * 512], BF16, side="right"))
    ub_sb = E4(nc.sbuf_tensor("ub_sb", [128, 2 * 512], BF16, side="right"))
    mot_sb = E4(nc.sbuf_tensor("mot_sb", [128, 1024], F32, side="right"))
    crep = E4(nc.sbuf_tensor("crep", [128, 1024], BF16, side="right"))
    crow = E4(nc.sbuf_tensor("crow", [1, 2 * TOK], BF16, side="right"))
    actp = E4(nc.sbuf_tensor("actp", [128, NOC * 512], BF16, side="right"))
    mo_sb = E4(nc.sbuf_tensor("mo_sb", [128, NEC * TOK], F32, side="right"))
    h1f4 = E4(nc.sbuf_tensor("h1f4", [128, E], F32, side="right"))
    motm = E4(nc.sbuf_tensor("motm", [128, E], F32, side="right"))
    outb = E4(nc.sbuf_tensor("outb", [128, E], F32, side="right"))
    sq4 = E4(nc.sbuf_tensor("sq4", [128, E], F32, side="right"))
    ln4 = E4(nc.sbuf_tensor("ln4", [128, 8], F32, side="right"))
    pgu0 = E4(nc.psum_tensor("pgu0", [128, 1024], F32))
    pgu1 = E4(nc.psum_tensor("pgu1", [128, 1024], F32))
    pmo0 = E4(nc.psum_tensor("pmo0", [128, 1024], F32))
    pmo1 = E4(nc.psum_tensor("pmo1", [128, 1024], F32))
    if 1:
        pgus = (pgu0, pgu1)
        pmos = (pmo0, pmo1)

        gumm = {}
        crm_t = {}
        silu_t = {}
        tmp_t = {}
        ap_t = {}
        w2mm = {}
        moadd = {}
        crev = {}
        crl_t = {}
        for e in range(NE):
            epar = e % 2
            if e >= 2:
                b.wait("sync", s_mm, crm_t[e - 2])
            crl_t[e] = b.dma("sync", dsem[6 + epar], crow[0:1, epar * TOK:(epar + 1) * TOK],
                             cT_dram[e:e + 1, :])
            if e >= 1:
                b.wait("tensor", s_mo, moadd[(e - 1, 14)])
            b.wait("tensor", dsem[6 + epar], crl_t[e])
            for tt in range(2):
                kw = dict(start=True, stop=True)
                if tt == 1:
                    kw["sem"] = s_mm
                b.op("tensor", "matmul", pmos[0][:, tt * 512:(tt + 1) * 512],
                     onesr[0:1, 0:128],
                     crow[0:1, epar * TOK + tt * 512: epar * TOK + (tt + 1) * 512], **kw)
            crm = s_mm.n
            crm_t[e] = crm
            b.wait("scalar", s_mm, crm)
            crev[e] = b.op("scalar", "copy", crep[:], pmos[0][:, 0:1024],
                           sem=s_ve)

            for p in range(NOC):
                ic = p // 2
                is_g = (p % 2 == 0)
                oc = ic if is_g else NIC + ic
                gp = e * NOC + p
                wpar = p % 2
                if gp >= 2:
                    b.wait("gpsimd", s_mm, gumm[gp - 2])
                for sp in range(4):
                    wt = b.dma(
                        "gpsimd", swsem[wpar],
                        w1col[:, (wpar * NEC + sp * 4) * 128:(wpar * NEC + (sp + 1) * 4) * 128].rearrange(
                            "p (ec f) -> p ec f", ec=4),
                        dw1t[e, sp * 512:(sp + 1) * 512, oc * 128:(oc + 1) * 128].rearrange(
                            "(ec p) f -> p ec f", p=128))
                b.wait("tensor", swsem[wpar], wt)
                if gp >= 2:
                    pic, pe = (ic - 1, e) if ic >= 1 else (NIC - 1, e - 1)
                    b.wait("tensor", s_ybf, silu_t[(pe, pic, 1)])
                for tt in range(2):
                    for ec in range(NEC):
                        kw = dict(start=(ec == 0), stop=(ec == NEC - 1))
                        if tt == 1 and ec == NEC - 1:
                            kw["sem"] = s_mm
                        b.op("tensor", "matmul", pgus[wpar][:, tt * 512:(tt + 1) * 512],
                             w1col[:, (wpar * NEC + ec) * 128:(wpar * NEC + ec + 1) * 128],
                             h1T[:, ec * TOK + tt * 512: ec * TOK + tt * 512 + 512],
                             **kw)
                gumm[gp] = s_mm.n
                if is_g:
                    b.wait("scalar", s_mm, gumm[gp])
                    if ic >= 2 or e >= 1:
                        pic, pe = (ic - 2, e) if ic >= 2 else (NIC - 2 + ic, e - 1)
                        b.wait("scalar", s_ap, ap_t[(pe, pic, 1)])
                    for tt in range(2):
                        silu_t[(e, ic, tt)] = b.op(
                            "scalar", "activation",
                            silu_sb[:, ((ic % 2) * 2 + tt) * 512:((ic % 2) * 2 + tt + 1) * 512],
                            pgus[wpar][:, tt * 512:(tt + 1) * 512], AF.Silu,
                            sem=s_ybf)
                else:
                    b.wait("scalar", s_mm, gumm[gp])
                    if ic >= 1 or e >= 1:
                        pic2, pe2 = (ic - 1, e) if ic >= 1 else (NIC - 1, e - 1)
                        b.wait("scalar", s_ve, tmp_t[(pe2, pic2, 1)])
                    uev = {}
                    for tt in range(2):
                        uev[tt] = b.op(
                            "scalar", "copy", ub_sb[:, tt * 512:(tt + 1) * 512],
                            pgus[wpar][:, tt * 512:(tt + 1) * 512], sem=s_ybf)
                    b.wait("vector", s_ve, crev[e])
                    for tt in range(2):
                        b.wait("vector", s_ybf, uev[tt])
                        if ic >= 1 or e >= 1:
                            pic2, pe2 = (ic - 1, e) if ic >= 1 else (NIC - 1, e - 1)
                            b.wait("vector", s_ap, ap_t[(pe2, pic2, tt)])
                        tmp_t[(e, ic, tt)] = b.op(
                            "vector", "tensor_tensor", tmpb[:, tt * 512:(tt + 1) * 512],
                            ub_sb[:, tt * 512:(tt + 1) * 512],
                            crep[:, tt * 512:(tt + 1) * 512], ALU.mult, sem=s_ve)
                    for tt in range(2):
                        b.wait("gpsimd", s_ve, tmp_t[(e, ic, tt)])
                        b.wait("gpsimd", s_ybf, silu_t[(e, ic, tt)])
                        if e >= 1:
                            b.wait("gpsimd", s_mm, w2mm[(e - 1, 15)])
                        ap_t[(e, ic, tt)] = b.op(
                            "gpsimd", "tensor_tensor",
                            actp[:, (tt * NIC + ic) * 512:(tt * NIC + ic + 1) * 512],
                            tmpb[:, tt * 512:(tt + 1) * 512],
                            silu_sb[:, ((ic % 2) * 2 + tt) * 512:((ic % 2) * 2 + tt + 1) * 512],
                            ALU.mult, sem=s_ap)

            for hc in range(NEC):
                wpar2 = hc % 2
                if (e, hc - 2) in w2mm:
                    b.wait("gpsimd", s_mm, w2mm[(e, hc - 2)])
                elif e >= 1:
                    b.wait("gpsimd", s_mm, w2mm[(e - 1, 14 + hc)])
                for sp in range(2):
                    nic_sp = min((sp + 1) * 6, NIC) - sp * 6
                    wt2 = b.dma(
                        "gpsimd", swsem[wpar2],
                        w2col[:, (wpar2 * NIC + sp * 6) * 128:
                              (wpar2 * NIC + min((sp + 1) * 6, NIC)) * 128].rearrange(
                            "p (ic f) -> p ic f", ic=nic_sp),
                        dw2t[e, sp * 768:min((sp + 1) * 768, I1),
                             hc * 128:(hc + 1) * 128].rearrange(
                            "(ic p) f -> p ic f", p=128))
                b.wait("tensor", swsem[wpar2], wt2)
                if hc == 0:
                    b.wait("tensor", s_ap, ap_t[(e, NIC - 1, 1)])
                    b.wait("tensor", s_ve, crev[e])
                if (e, hc - 2) in moadd:
                    b.wait("tensor", s_mo, moadd[(e, hc - 2)])
                elif e >= 1 and hc >= 1:
                    b.wait("tensor", s_mo, moadd[(e - 1, 14 + hc - 1)])
                for tt in range(2):
                    for ic in range(NIC):
                        kw = dict(start=(ic == 0), stop=(ic == NIC - 1))
                        if tt == 1 and ic == NIC - 1:
                            kw["sem"] = s_mm
                        b.op("tensor", "matmul",
                             pmos[wpar2][:, tt * 512:(tt + 1) * 512],
                             w2col[:, (wpar2 * NIC + ic) * 128:(wpar2 * NIC + ic + 1) * 128],
                             actp[:, (tt * NIC + ic) * 512:(tt * NIC + ic + 1) * 512],
                             **kw)
                w2mm[(e, hc)] = s_mm.n
                b.wait("scalar", s_mm, w2mm[(e, hc)])
                if (e, hc) != (0, 0):
                    b.wait("scalar", s_mo, s_mo.n)
                mev = b.op("scalar", "copy", mot_sb[:, 0:1024],
                           pmos[wpar2][:, 0:1024], sem=s_ybf)
                b.wait("vector", s_ybf, mev)
                for tt in range(2):
                    kw = {}
                    if tt == 1:
                        kw["sem"] = s_mo
                    dstm = mo_sb[:, hc * TOK + tt * 512: hc * TOK + tt * 512 + 512]
                    if e == 0:
                        b.op("vector", "tensor_copy", dstm,
                             mot_sb[:, tt * 512:(tt + 1) * 512], **kw)
                    else:
                        b.op("vector", "tensor_tensor", dstm, dstm,
                             mot_sb[:, tt * 512:(tt + 1) * 512], ALU.add, **kw)
                moadd[(e, hc)] = s_mo.n

        ln2_t = {}
        mev_t = {}
        ost_t = {}
        for a in range(8):
            if a >= 1:
                b.wait("sync", s_ln, ln2_t[a - 1])
            h1ld = b.dma("sync", dsem[a % 2], h1f4[:, 0:E],
                         h1_dram[a * 128:(a + 1) * 128, :])
            if a == 0:
                b.wait("tensor", s_mo, moadd[(NE - 1, NEC - 1)])
                b.wait("tensor", s_ybf, s_ybf.n)
            for hc in range(NEC):
                kk = a * NEC + hc
                pg = pgus[kk % 2]
                col = ((kk // 2) % 2) * 512
                if kk >= 4:
                    b.wait("tensor", s_xT, mev_t[kk - 4])
                t = b.op("tensor", "transpose", pg[:, col:col + 128],
                         mo_sb[:, hc * TOK + a * 128: hc * TOK + (a + 1) * 128],
                         id_f[:], sem=s_tp)
                b.wait("scalar", s_tp, t)
                if a >= 1 and hc == 0:
                    b.wait("scalar", s_ln, ln2_t[a - 1])
                mev_t[kk] = b.op(
                    "scalar", "copy",
                    motm[:, hc * 128:(hc + 1) * 128],
                    pg[:, col:col + 128], sem=s_xT)
            b.wait("vector", dsem[a % 2], h1ld)
            b.wait("vector", s_xT, mev_t[a * NEC + NEC - 1])
            if a >= 1:
                b.wait("vector", s_out, ost_t[a - 1])
            st_t = b.op("vector", "scalar_tensor_tensor", outb[:, 0:E],
                        h1f4[:, 0:E], 1.0, motm[:, 0:E],
                        ALU.mult, ALU.add, sem=s_ln)
            b.wait("vector", s_ln, st_t)
            rd4_t = b.op("vector", "reduce_sum", ln4[:, 0:1], outb[:, 0:E], AX.X,
                         sem=s_ln)
            b.wait("vector", s_ln, rd4_t)
            mn4_t = b.op("vector", "tensor_scalar", ln4[:, 1:2], ln4[:, 0:1], 1.0 / E,
                         None, ALU.mult, sem=s_ln)
            b.wait("vector", s_ln, mn4_t)
            b.op("vector", "tensor_scalar", motm[:, 0:E],
                 outb[:, 0:E], ln4[:, 1:2], None, ALU.subtract)
            b.op("vector", "scalar_tensor_tensor", sq4[:, 0:E],
                 motm[:, 0:E], 1.0, motm[:, 0:E],
                 ALU.mult, ALU.mult)
            sq_t = b.op("vector", "reduce_sum", ln4[:, 2:3], sq4[:, 0:E], AX.X, sem=s_ln)
            b.wait("scalar", s_ln, sq_t)
            lnv4_t = b.op("scalar", "activation", ln4[:, 3:4], ln4[:, 2:3], AF.Ln,
                          bias=eps_sb[:, 0:1], scale=1.0 / E, sem=s_act)
            b.wait("scalar", s_act, lnv4_t)
            rs_t = b.op("scalar", "activation", ln4[:, 4:5], ln4[:, 3:4], AF.Exp,
                        scale=-0.5, sem=s_act)
            b.wait("vector", s_act, rs_t)
            o_t = b.op("vector", "tensor_scalar", outb[:, 0:E],
                       motm[:, 0:E], ln4[:, 4:5], None, ALU.mult,
                       sem=s_ln)
            ln2_t[a] = o_t
            b.wait("sync", s_ln, o_t)
            ost_t[a] = b.dma("sync", s_out, dout[a * 128:(a + 1) * 128, :],
                             outb[:, 0:E])

        if "p3" in DEBUG_TAPS:
            dh1 = dbg_out("dbg_h1", [TOK, E])
            dct = dbg_out("dbg_cT", [8, TOK], BF16)
            b.gate(engs=("sync",))
            b.dma("sync", s_st, dh1[:, :], h1_dram[:, :])
            b.dma("sync", s_st, dct[:, :], cT_dram[:, :])
        if "p4" in DEBUG_TAPS:
            dmo = dbg_out("dbg_mo", [128, NEC * TOK])
            b.gate(engs=("sync",))
            b.dma("sync", s_st, dmo[:, :], mo_sb[:])
        b.wait("sync", s_out, 8 * 16)
        b.wait("sync", s_st, s_st.n)

    p4s.close()
    b.audit()
    b.emit(cut=cut)
    st.close()
    return nc, dbg


def tp3_last(k, otm_t):
    return otm_t[8 * (k // 2) + 4 * (k % 2) + 3][0]


_NC_CACHE = None


def _get_nc():
    global _NC_CACHE
    if _NC_CACHE is None:
        _NC_CACHE = build_nc()
    return _NC_CACHE


def kernel(**kw):
    x = np.ascontiguousarray(np.asarray(kw["L0_attn_x"], dtype=np.float32).reshape(8192, E))
    Wq = np.asarray(kw["L0_attn_Wq"], np.float32)
    bq = np.asarray(kw["L0_attn_bq"], np.float32)
    Wk = np.asarray(kw["L0_attn_Wk"], np.float32)
    bk = np.asarray(kw["L0_attn_bk"], np.float32)
    Wv = np.asarray(kw["L0_attn_Wv"], np.float32)
    bv = np.asarray(kw["L0_attn_bv"], np.float32)
    Wo = np.asarray(kw["L0_attn_Wo"], np.float32)
    bo = np.asarray(kw["L0_attn_bo"], np.float32)
    gw = np.asarray(kw["L0_ffn_gate_w"], np.float32)
    w1 = np.asarray(kw["L0_ffn_w1"], np.float32)
    w2 = np.asarray(kw["L0_ffn_w2"], np.float32)

    perm_head = np.concatenate([np.arange(0, 128, 2), np.arange(1, 128, 2)])
    perm = (np.arange(H)[:, None] * 128 + perm_head[None, :]).reshape(-1)

    common = {
        "wqt": np.ascontiguousarray(Wq.T[:, perm]),
        "wkt": np.ascontiguousarray(Wk.T[:, perm]),
        "wvt": np.ascontiguousarray(Wv.T),
        "wot": np.ascontiguousarray(Wo.T),
        "bq": np.ascontiguousarray(bq[perm]).reshape(1, E),
        "bk": np.ascontiguousarray(bk[perm]).reshape(1, E),
        "bv": bv.reshape(1, E).copy(),
        "bot": np.ascontiguousarray(bo.reshape(NEC, 128).T),
        "gwt": np.ascontiguousarray(gw.T.reshape(NEC, 128, NE).transpose(1, 0, 2).reshape(128, 128)),
        "w1t": np.ascontiguousarray(w1.transpose(0, 2, 1)),
        "w2t": np.ascontiguousarray(w2.transpose(0, 2, 1)),
    }

    j64 = np.arange(64, dtype=np.float64)
    theta = 10000.0 ** (-2.0 * j64 / 128.0)

    in_maps = []
    for c in range(NCORE):
        rows = np.concatenate(
            [np.arange(256 * (8 * j + c), 256 * (8 * j + c) + 256) for j in range(4)])
        ang = rows[None, :].astype(np.float64) * theta[:, None]
        cos_t = np.cos(ang)
        sin_t = np.sin(ang)
        thr = np.zeros((128, NSLOT_TOT), np.float32)
        s = 0
        for j in range(4):
            qb = 256 * (8 * j + c)
            for i in range(16 * (j + 1)):
                thr[:, s] = qb - 128 * i
                s += 1
        m = dict(common)
        m["x"] = np.ascontiguousarray(x[rows])
        m["cosx"] = np.ascontiguousarray(
            np.concatenate([cos_t, cos_t], 0).astype(np.float32))
        m["sinx"] = np.ascontiguousarray(
            np.concatenate([-sin_t, sin_t], 0).astype(np.float32))
        m["thr"] = thr
        in_maps.append(m)

    nc, _dbg = _get_nc()
    res = run_bass_kernel_spmd(nc, in_maps, core_ids=list(range(NCORE)))
    out = np.empty((8192, E), np.float32)
    for c in range(NCORE):
        oc = np.asarray(res.results[c]["out"])
        for j in range(4):
            bidx = 8 * j + c
            out[256 * bidx:256 * bidx + 256] = oc[256 * j:256 * j + 256]
    return out.reshape(1, 8192, E)
